# version 03
# baseline (speedup 1.0000x reference)
import numpy as np
import ml_dtypes
import concourse.bass as bass
import concourse.mybir as mybir
from concourse.bass_utils import run_bass_kernel_spmd

F32 = mybir.dt.float32
BF16 = mybir.dt.bfloat16
AF = mybir.ActivationFunctionType
ALU = mybir.AluOpType
AX = mybir.AxisListType

D = 2048
S = 4096
NQ = 2048
H = 16
HD = 128
NE = 8
RK = 64
BETA = 9.5
EPS = 1e-5

ENGS = ("tensor", "scalar", "vector", "sync", "gpsimd")


class Sem:
    def __init__(self, h, name):
        self.h = h
        self.name = name
        self.n = 0


class Rec:
    def __init__(self):
        self.ops = {e: [] for e in ENGS}
        self.waits = []

    def wait(self, e, sem, val):
        if val <= 0:
            return
        self.waits.append((sem, val))
        self.ops[e].append(lambda eng, s=sem, v=val: eng.wait_ge(s.h, v))

    def op(self, e, build, incs=()):
        incs = tuple(incs)
        for s, a in incs:
            s.n += a
        assert len(incs) <= 1, "one sem update per instruction"
        attach, spill = incs[:1], incs[1:]

        def f(eng, build=build, attach=attach, spill=spill):
            inst = build(eng)
            for s, a in attach:
                inst = inst.then_inc(s.h, a)
            for s, a in spill:
                eng.sem_inc(s.h, a)
        self.ops[e].append(f)

    def check(self):
        for s, v in self.waits:
            assert v <= s.n, f"wait on {s.name} for {v} but total incs {s.n}"


def dap(t, off, dims):
    return bass.AP(t, off, [list(d) for d in dims])


def build_nc():
    nc = bass.Bass("TRN2", target_bir_lowering=False)

    xTb = nc.dram_tensor("xTb", [D, S], F32, kind="ExternalInput")
    xnat = nc.dram_tensor("xnat", [NQ, D], F32, kind="ExternalInput")
    wqT = nc.dram_tensor("wqT", [D, D], F32, kind="ExternalInput")
    wkT = nc.dram_tensor("wkT", [D, D], F32, kind="ExternalInput")
    wvT = nc.dram_tensor("wvT", [D, D], F32, kind="ExternalInput")
    woT = nc.dram_tensor("woT", [D, D], F32, kind="ExternalInput")
    ATd = nc.dram_tensor("ATd", [D, NE * RK], F32, kind="ExternalInput")
    BTd = nc.dram_tensor("BTd", [4 * 128, D], F32, kind="ExternalInput")
    WrTd = nc.dram_tensor("WrTd", [D, NE], F32, kind="ExternalInput")
    brb = nc.dram_tensor("brb", [128, NE], F32, kind="ExternalInput")
    identd = nc.dram_tensor("identd", [128, 128], BF16, kind="ExternalInput")
    onesmd = nc.dram_tensor("onesmd", [128, 128], F32, kind="ExternalInput")
    onesmbd = nc.dram_tensor("onesmbd", [128, 128], BF16, kind="ExternalInput")
    onesrd = nc.dram_tensor("onesrd", [1, 128], BF16, kind="ExternalInput")

    y = nc.dram_tensor("y", [NQ, D], F32, kind="ExternalOutput")

    kdram = nc.dram_tensor("kdram", [8, 128, 8192], BF16)
    vdram = nc.dram_tensor("vdram", [S, D], BF16)
    qdram = nc.dram_tensor("qdram", [4, 128, 8192], BF16)
    negdram = nc.dram_tensor("negdram", [16, 2048], BF16)

    from contextlib import ExitStack
    ctx = ExitStack()
    e = ctx.enter_context

    AB = [e(nc.sbuf_tensor(f"AB{i}", [128, 8192], BF16)) for i in range(4)]
    A_ = [AB[i // 4][:, (i % 4) * 2048:(i % 4 + 1) * 2048] for i in range(16)]
    B_ = [e(nc.sbuf_tensor(f"B{i}", [128, 8192], BF16)) for i in range(4)]
    C_ = [e(nc.sbuf_tensor(f"C{i}", [128, 8192], BF16)) for i in range(3)]
    obuf = [e(nc.sbuf_tensor(f"obuf{i}", [128, 2048], F32)) for i in range(2)]
    mh = e(nc.sbuf_tensor("mh", [128, 1024], BF16))
    coefT = e(nc.sbuf_tensor("coefT", [128, 128], F32))
    stcol = e(nc.sbuf_tensor("stcol", [128, 64], F32))
    smf = e(nc.sbuf_tensor("smf", [128, 32], F32))
    lgts = e(nc.sbuf_tensor("lgts", [128, 16], F32))
    wrtb = e(nc.sbuf_tensor("wrtb", [128, 128], BF16))
    brs = e(nc.sbuf_tensor("brs", [128, NE], F32))
    identb = e(nc.sbuf_tensor("identb", [128, 128], BF16))
    onesm = e(nc.sbuf_tensor("onesm", [128, 128], F32))
    onesr = e(nc.sbuf_tensor("onesr", [1, 128], BF16))
    onesmb = e(nc.sbuf_tensor("onesmb", [128, 128], BF16))
    acc2 = e(nc.sbuf_tensor("acc2", [128, 2048], BF16))
    X = e(nc.sbuf_tensor("X", [128, 2048], BF16))

    psA = e(nc.psum_tensor("psA", [128, 2048], F32))
    psB_cm = nc.psum_tensor("psB", [128, 2048], F32)
    psB = psB_cm.__enter__()

    semnames = ("wA wAf x0 x1 xf0 xf1 mmP1 evP1 sp kv0 kv1 kvf0 kvf1 sc exp avf accv "
                "evat d rec bc nrm c0 c1 c2 cf0 cf1 cf2 op r1 st1 sq1 vch sch n1 tx "
                "asc aexp aaccv ad aln alnc ng ngl vmm vev vsp wv "
                "txev txes wF h lg co mk yy x2 n2 out ld0 ld0g").split()
    S_ = {n: Sem(e(nc.semaphore(f"s_{n}")), n) for n in semnames}

    r = Rec()

    for dst, src in ((identb, identd), (onesm, onesmd), (onesmb, onesmbd), (onesr, onesrd), (brs, brb)):
        r.wait("sync", S_["ld0"], S_["ld0"].n)
        r.op("sync", lambda eng, d=dst, s=src: eng.dma_start(d[:, :], s.ap()[:, :]),
             [(S_["ld0"], 16)])
    r.op("gpsimd", lambda eng: eng.dma_start(
        wrtb[:, :], dap(WrTd, 0, [[NE, 128], [128 * NE, 16], [1, NE]])), [(S_["ld0g"], 16)])
    lds0 = S_["ld0"].n

    def pool6(i):
        i = i % 6
        if i < 4:
            return psA[:, i * 512:(i + 1) * 512]
        return psB[:, (i - 4) * 512:(i - 4 + 1) * 512]

    g_grp = 0
    n_spills = 0
    vslot_i = 0
    stage_last_spill = {}
    xcnk_free_at = {}
    pass_end_mm = []

    passes = (("k", wkT, 8, kdram), ("q", wqT, 4, qdram))
    for p_i, (kind, wT, nchunks, dest) in enumerate(passes):
        for g4 in range(4):
            if g4 == 0 and p_i > 0:
                r.wait("gpsimd", S_["mmP1"], pass_end_mm[p_i - 1])
            wsem = S_["wA"] if g4 % 2 == 0 else S_["wAf"]
            r.wait("gpsimd", wsem, wsem.n)
            r.op("gpsimd", lambda eng, g4=g4, wT=wT: eng.dma_start(
                AB[g4][:, :],
                dap(wT, g4 * 4 * 128 * D, [[D, 128], [128 * D, 4], [1, D]])),
                [(wsem, 16)])
        wA_after = (S_["wA"].n, S_["wAf"].n)

        for t in range(nchunks):
            par = t % 2
            if par in xcnk_free_at:
                r.wait("gpsimd", S_["mmP1"], xcnk_free_at[par])
            r.op("gpsimd", lambda eng, par=par, t=t: eng.dma_start(
                C_[par][:, :], dap(xTb, t * 512, [[S, 128], [128 * S, 16], [1, 512]])),
                [(S_["x" + str(par)], 16)])
            x_after = S_["x" + str(par)].n

            r.wait("tensor", S_["wA"], wA_after[0])
            r.wait("tensor", S_["wAf"], wA_after[1])
            r.wait("tensor", S_["x" + str(par)], x_after)

            if True:
                for dk in range(16):
                    slot = g_grp % 6
                    r.wait("tensor", S_["evP1"], g_grp - 5)
                    for mc in range(16):
                        incs = [(S_["mmP1"], 1)] if mc == 15 else []
                        r.op("tensor", lambda eng, slot=slot, mc=mc, dk=dk, par=par:
                             eng.matmul(pool6(slot),
                                        A_[mc][:, dk * 128:(dk + 1) * 128],
                                        C_[par][:, mc * 512:(mc + 1) * 512],
                                        start=(mc == 0), stop=(mc == 15)), incs)
                    r.wait("scalar", S_["mmP1"], S_["mmP1"].n)
                    if dk == 0 and ("kq", par) in stage_last_spill:
                        r.wait("scalar", S_["sp"], 16 * stage_last_spill[("kq", par)])
                    r.op("scalar", lambda eng, slot=slot, dk=dk, par=par:
                         eng.copy(B_[par][:, dk * 512:(dk + 1) * 512], pool6(slot)),
                         [(S_["evP1"], 1)])
                    g_grp += 1
                r.wait("sync", S_["evP1"], S_["evP1"].n)
                r.wait("sync", S_["sp"], S_["sp"].n)
                r.op("sync", lambda eng, t=t, par=par, dest=dest:
                     eng.dma_start(dest.ap()[t, :, :], B_[par][:, :]), [(S_["sp"], 16)])
                n_spills += 1
                stage_last_spill[("kq", par)] = n_spills
            xcnk_free_at[par] = S_["mmP1"].n
        pass_end_mm.append(S_["mmP1"].n)

    P1_SPILLS = S_["sp"].n
    P1_MM = S_["mmP1"].n

    acc = obuf[0]

    def ktbuf(par, piece):
        return A_[2 * par + piece]

    def vbuf(par, piece):
        return A_[4 + 2 * par + piece]

    def qbuf(par):
        return A_[8 + par]

    def expT(par):
        return A_[10 + par]

    def attnT(h):
        return B_[h // 4][:, (h % 4) * 2048:(h % 4 + 1) * 2048]

    NEB = 6

    r.wait("tensor", S_["ld0"], lds0)
    pendingA = None

    def emit_d9(hh, rnd):
        if rnd == 0:
            r.wait("tensor", S_["aaccv"], 32 * (hh + 1))
        r.wait("tensor", S_["aln"], 2 * hh + rnd)
        for j in range(2):
            qc = 2 * rnd + j
            r.op("tensor", lambda eng, qc=qc, j=j:
                 eng.matmul(psB[:, 1024 + j * 512:1024 + (j + 1) * 512], onesmb[:, :],
                            acc2[:, qc * 512:(qc + 1) * 512], start=True, stop=True),
                 [(S_["ad"], 1)] if j == 1 else [])

    v_groups = [(dvc, chunk, tt) for dvc in range(4) for chunk in range(8)
                for tt in range(4)]
    v_i = 0
    vx_loads = {}
    vx_after = {}
    wv_after = {}
    v_emitted_loads = set()

    def emit_v_loads(dvc, chunk):
        if dvc not in wv_after:
            if dvc > 0:
                r.wait("gpsimd", S_["vmm"], 32 * dvc)
            r.op("gpsimd", lambda eng, dvc=dvc: eng.dma_start(
                C_[2][:, :], dap(wvT, dvc * 512,
                                 [[D, 128], [128 * D, 16], [1, 512]])), [(S_["wv"], 16)])
            wv_after[dvc] = S_["wv"].n
        par = chunk % 2
        key = (dvc, chunk)
        if key not in v_emitted_loads:
            if par in vx_loads:
                r.wait("gpsimd", S_["vmm"], vx_loads[par])
            else:
                r.wait("gpsimd", S_["mmP1"], P1_MM)
            r.op("gpsimd", lambda eng, chunk=chunk: eng.dma_start(
                C_[par][:, :], dap(xTb, chunk * 512, [[S, 128], [128 * S, 16], [1, 512]])),
                [(S_["x" + str(par)], 16)])
            vx_after[key] = S_["x" + str(par)].n
            v_emitted_loads.add(key)

    v_iv = 0
    v_is = 0

    def emit_v_group_tensor():
        nonlocal v_i
        if v_i >= len(v_groups):
            return
        dvc, chunk, tt = v_groups[v_i]
        par = chunk % 2
        vg = v_i
        emit_v_loads(dvc, chunk)
        if v_i + 1 < len(v_groups):
            emit_v_loads(v_groups[v_i + 1][0], v_groups[v_i + 1][1])
        r.wait("tensor", S_["x" + str(par)], vx_after[(dvc, chunk)])
        r.wait("tensor", S_["wv"], wv_after[dvc])
        r.wait("tensor", S_["vev"], vg - 1)
        bank = vg % 2
        for mc in range(16):
            r.op("tensor", lambda eng, mc=mc, tt=tt, par=par, bank=bank:
                 eng.matmul(psB[:, bank * 512:(bank + 1) * 512],
                            C_[par][:, mc * 512 + tt * 128: mc * 512 + (tt + 1) * 128],
                            C_[2][:, mc * 512:(mc + 1) * 512],
                            start=(mc == 0), stop=(mc == 15)),
                 [(S_["vmm"], 1)] if mc == 15 else [])
        if tt == 3:
            vx_loads[par] = vg + 1
        v_i += 1

    def emit_v_group_vector():
        nonlocal v_iv
        if v_iv >= len(v_groups):
            return
        vg = v_iv
        slot = vg % 8
        bank = vg % 2
        r.wait("vector", S_["vmm"], vg + 1)
        if vg >= 8:
            r.wait("vector", S_["vsp"], 16 * (vg - 7))
        r.op("vector", lambda eng, slot=slot, bank=bank:
             eng.tensor_copy(B_[2][:, slot * 512:(slot + 1) * 512],
                             psB[:, bank * 512:(bank + 1) * 512]), [(S_["vev"], 1)])
        v_iv += 1

    def emit_v_group_sync():
        nonlocal v_is
        if v_is >= len(v_groups):
            return
        dvc, chunk, tt = v_groups[v_is]
        vg = v_is
        slot = vg % 8
        r.wait("sync", S_["vev"], vg + 1)
        r.wait("sync", S_["vsp"], S_["vsp"].n)
        row0 = (chunk * 4 + tt) * 128
        r.op("sync", lambda eng, row0=row0, dvc=dvc, slot=slot:
             eng.dma_start(vdram.ap()[row0:row0 + 128, dvc * 512:(dvc + 1) * 512],
                           B_[2][:, slot * 512:(slot + 1) * 512]), [(S_["vsp"], 16)])
        v_is += 1

    for h in range(H):
        par = h % 2
        if h == 0:
            r.wait("sync", S_["sp"], P1_SPILLS)
            r.wait("sync", S_["mmP1"], P1_MM)
        if h >= 2:
            r.wait("sync", S_["asc"], 64 * (h - 1))
        for piece in range(2):
            r.wait("sync", S_["kv" + str(par)], S_["kv" + str(par)].n)
            r.op("sync", lambda eng, h=h, g=piece, par=par: eng.dma_start(
                ktbuf(par, g)[:, :],
                dap(kdram, (4 * g) * 128 * 8192 + h * 512,
                    [[8192, 128], [128 * 8192, 4], [1, 512]])), [(S_["kv" + str(par)], 16)])
        r.wait("sync", S_["kv" + str(par)], S_["kv" + str(par)].n)
        r.op("sync", lambda eng, h=h, par=par: eng.dma_start(
            qbuf(par)[:, :],
            dap(qdram, h * 512, [[8192, 128], [128 * 8192, 4], [1, 512]])),
            [(S_["kv" + str(par)], 16)])
        if h >= 1:
            r.wait("sync", S_["alnc"], 2 * h)
            r.op("sync", lambda eng, hh=h - 1: eng.dma_start(
                negdram.ap()[hh:hh + 1, :], X[0:1, :]), [(S_["ng"], 16)])
        for _ in range(8):
            emit_v_group_sync()
        kv_after = S_["kv" + str(par)].n

        r.wait("tensor", S_["kv" + str(par)], kv_after)
        for kt in range(32):
            gkt = h * 32 + kt
            piece, tile = kt // 16, kt % 16
            for half in range(2):
                if gkt >= 1:
                    r.wait("tensor", S_["aexp"], 2 * (gkt - 1) + half + 1)
                for sub in range(2):
                    qc = half * 1024 + sub * 512
                    incs = [(S_["asc"], 1)] if sub == 1 else []
                    r.op("tensor", lambda eng, piece=piece, tile=tile, qc=qc, par=par:
                         eng.matmul(psA[:, qc:qc + 512],
                                    ktbuf(par, piece)[:, tile * 128:(tile + 1) * 128],
                                    qbuf(par)[:, qc:qc + 512],
                                    start=True, stop=True), incs)
            if kt == 0 and pendingA is not None:
                emit_d9(pendingA, 0)
            if kt == 4 and pendingA is not None:
                emit_d9(pendingA, 1)
                pendingA = None
            if gkt % 4 == 1:
                emit_v_group_tensor()
        pendingA = h
    emit_d9(pendingA, 0)
    emit_d9(pendingA, 1)
    while v_i < len(v_groups):
        emit_v_group_tensor()
    while v_is < len(v_groups):
        emit_v_group_sync()
    r.wait("sync", S_["alnc"], 32)
    r.op("sync", lambda eng: eng.dma_start(negdram.ap()[15:16, :], X[0:1, :]),
         [(S_["ng"], 16)])

    def emit_ln(hh, rnd):
        r.wait("scalar", S_["ad"], 2 * hh + rnd + 1)
        r.wait("scalar", S_["alnc"], 2 * hh + rnd - 1)
        r.op("scalar", lambda eng, rnd=rnd: eng.activation(
            X[64:65, rnd * 1024:(rnd + 1) * 1024],
            psB[0:1, 1024:2048], AF.Ln), [(S_["aln"], 1)])

    for h in range(H):
        for kt in range(32):
            gkt = h * 32 + kt
            for half in range(2):
                r.wait("scalar", S_["asc"], 2 * gkt + half + 1)
                if gkt >= NEB and half == 0:
                    r.wait("scalar", S_["aaccv"], gkt - NEB + 1)
                r.op("scalar", lambda eng, gkt=gkt, half=half:
                     eng.activation(expT(gkt % NEB)[:, half * 1024:(half + 1) * 1024],
                                    psA[:, half * 1024:(half + 1) * 1024],
                                    AF.Exp, bias=np.float32(-5.0),
                                    scale=np.float32(1.0 / BETA)), [(S_["aexp"], 1)])
            if h >= 1 and kt == 2:
                emit_ln(h - 1, 0)
            if h >= 1 and kt == 4:
                emit_ln(h - 1, 1)
    emit_ln(15, 0)
    emit_ln(15, 1)

    def emit_negrow(hh, rnd):
        r.wait("vector", S_["aln"], 2 * hh + rnd + 1)
        if rnd == 0:
            r.wait("vector", S_["ng"], 16 * hh)
        r.op("vector", lambda eng, rnd=rnd: eng.tensor_scalar(
            X[0:1, rnd * 1024:(rnd + 1) * 1024],
            X[64:65, rnd * 1024:(rnd + 1) * 1024],
            float(-BETA), float(-5.0 * BETA), ALU.mult, ALU.add), [(S_["alnc"], 1)])

    for h in range(H):
        for kt in range(32):
            gkt = h * 32 + kt
            r.wait("vector", S_["aexp"], 2 * gkt + 2)
            r.wait("vector", S_["aaccv"], S_["aaccv"].n)
            if kt == 0:
                r.wait("vector", S_["ad"], 2 * h)
                r.op("vector", lambda eng, gkt=gkt: eng.tensor_copy(acc2[:, :],
                                                                    expT(gkt % NEB)[:, :]),
                     [(S_["aaccv"], 1)])
            else:
                r.op("vector", lambda eng, gkt=gkt: eng.tensor_add(acc2[:, :], acc2[:, :],
                                                                   expT(gkt % NEB)[:, :]),
                     [(S_["aaccv"], 1)])
            if gkt % 4 == 1:
                emit_v_group_vector()
            if h >= 1 and kt == 6:
                emit_negrow(h - 1, 0)
            if h >= 1 and kt == 8:
                emit_negrow(h - 1, 1)
    emit_negrow(15, 0)
    emit_negrow(15, 1)
    while v_iv < len(v_groups):
        emit_v_group_vector()

    pendingB = None

    def emit_denB(hh):
        r.wait("tensor", S_["accv"], 32 * (hh + 1))
        r.wait("tensor", S_["evat"], 2 * (hh + 1))
        for qc in range(4):
            r.op("tensor", lambda eng, qc=qc:
                 eng.matmul(psB[:, qc * 512:(qc + 1) * 512], onesm[:, :],
                            acc[:, qc * 512:(qc + 1) * 512], start=True, stop=True),
                 [(S_["d"], 1)] if qc == 3 else [])

    def emit_avB(h, j, par):
        gktj = h * 32 + j
        piece, tile = j // 16, j % 16
        r.wait("tensor", S_["exp"], 2 * gktj + 2)
        if j == 0:
            r.wait("tensor", S_["rec"], h)
            if h == 0:
                r.wait("tensor", S_["aln"], 32)
                r.wait("tensor", S_["vev"], 128)
        for oc in range(4):
            incs = [(S_["avf"], 1)] if oc == 3 else []
            r.op("tensor", lambda eng, piece=piece, tile=tile, oc=oc, gktj=gktj, j=j, par=par:
                 eng.matmul(psB[:, oc * 512:(oc + 1) * 512],
                            vbuf(par, piece)[:, tile * 128:(tile + 1) * 128],
                            expT(gktj % NEB)[:, oc * 512:(oc + 1) * 512],
                            start=(j == 0), stop=(j == 31)), incs)

    for h in range(H):
        par = h % 2
        if h < 2:
            r.wait("sync", S_["asc"], 64 * (15 + h))
        else:
            r.wait("sync", S_["sc"], 64 * (h - 1))
            r.wait("sync", S_["avf"], 32 * (h - 1))
        r.wait("sync", S_["vsp"], 16 * 32 * (h // 4 + 1))
        for piece in range(2):
            r.wait("sync", S_["kv" + str(par)], S_["kv" + str(par)].n)
            r.op("sync", lambda eng, h=h, g=piece, par=par: eng.dma_start(
                ktbuf(par, g)[:, :],
                dap(kdram, (4 * g) * 128 * 8192 + h * 512,
                    [[8192, 128], [128 * 8192, 4], [1, 512]])), [(S_["kv" + str(par)], 16)])
            r.wait("sync", S_["kv" + str(par)], S_["kv" + str(par)].n)
            r.op("sync", lambda eng, h=h, g=piece, par=par: eng.dma_start(
                vbuf(par, g)[:, :],
                dap(vdram, (g * 16 * 128) * D + h * 128,
                    [[D, 128], [128 * D, 16], [1, 128]])), [(S_["kv" + str(par)], 16)])
        r.wait("sync", S_["kv" + str(par)], S_["kv" + str(par)].n)
        r.op("sync", lambda eng, h=h, par=par: eng.dma_start(
            qbuf(par)[:, :],
            dap(qdram, h * 512, [[8192, 128], [128 * 8192, 4], [1, 512]])),
            [(S_["kv" + str(par)], 16)])
        r.wait("sync", S_["ng"], 16 * (h + 1))
        if h >= 1:
            r.wait("sync", S_["sc"], 64 * h)
        r.wait("sync", S_["ngl"], S_["ngl"].n)
        r.op("sync", lambda eng, h=h: eng.dma_start(X[32:33, :], negdram.ap()[h:h + 1, :]),
             [(S_["ngl"], 16)])
        kv_after = S_["kv" + str(par)].n

        r.wait("tensor", S_["kv" + str(par)], kv_after)
        r.wait("tensor", S_["ngl"], S_["ngl"].n)
        if h == 0:
            r.wait("tensor", S_["aexp"], 1024)
        for kt in range(32):
            gkt = h * 32 + kt
            piece, tile = kt // 16, kt % 16
            for half in range(2):
                if gkt >= 1:
                    r.wait("tensor", S_["exp"], 2 * (gkt - 1) + half + 1)
                for sub in range(2):
                    qc = half * 1024 + sub * 512
                    r.op("tensor", lambda eng, qc=qc: eng.matmul(
                        psA[:, qc:qc + 512], onesmb[32:33, :], X[32:33, qc:qc + 512],
                        start=True, stop=False), [])
            for half in range(2):
                for sub in range(2):
                    qc = half * 1024 + sub * 512
                    incs = [(S_["sc"], 1)] if sub == 1 else []
                    r.op("tensor", lambda eng, piece=piece, tile=tile, qc=qc, par=par:
                         eng.matmul(psA[:, qc:qc + 512],
                                    ktbuf(par, piece)[:, tile * 128:(tile + 1) * 128],
                                    qbuf(par)[:, qc:qc + 512],
                                    start=False, stop=True), incs)
            if kt == 2 and pendingB is not None:
                emit_denB(pendingB)
                pendingB = None
            if kt >= NEB:
                emit_avB(h, kt - NEB, par)
        for j in range(32 - NEB, 32):
            emit_avB(h, j, par)
        pendingB = h
    emit_denB(pendingB)

    for h in range(H):
        for kt in range(32):
            gkt = h * 32 + kt
            for half in range(2):
                r.wait("scalar", S_["sc"], 2 * gkt + half + 1)
                if gkt < NEB:
                    if gkt == 0 and half == 0:
                        r.wait("scalar", S_["aaccv"], 512)
                elif half == 0:
                    r.wait("scalar", S_["avf"], gkt - NEB + 1)
                    r.wait("scalar", S_["accv"], gkt - NEB + 1)
                r.op("scalar", lambda eng, gkt=gkt, half=half:
                     eng.activation(expT(gkt % NEB)[:, half * 1024:(half + 1) * 1024],
                                    psA[:, half * 1024:(half + 1) * 1024],
                                    AF.Exp), [(S_["exp"], 1)])
        r.wait("scalar", S_["avf"], 32 * (h + 1))
        if h == 8:
            r.wait("scalar", S_["vsp"], 16 * 128)
        r.op("scalar", lambda eng, h=h: eng.copy(attnT(h)[:, 0:1024], psB[:, 0:1024]),
             [(S_["evat"], 1)])

    for h in range(H):
        for kt in range(32):
            gkt = h * 32 + kt
            r.wait("vector", S_["exp"], 2 * gkt + 2)
            r.wait("vector", S_["accv"], S_["accv"].n)
            if kt == 0:
                r.wait("vector", S_["d"], h)
                r.op("vector", lambda eng, gkt=gkt: eng.tensor_copy(acc[:, :],
                                                                    expT(gkt % NEB)[:, :]),
                     [(S_["accv"], 1)])
            else:
                r.op("vector", lambda eng, gkt=gkt: eng.tensor_add(acc[:, :], acc[:, :],
                                                                   expT(gkt % NEB)[:, :]),
                     [(S_["accv"], 1)])
        r.wait("vector", S_["avf"], 32 * (h + 1))
        if h == 0:
            r.wait("vector", S_["sp"], P1_SPILLS)
        if h == 8:
            r.wait("vector", S_["vsp"], 16 * 128)
        r.op("vector", lambda eng, h=h: eng.tensor_copy(attnT(h)[:, 1024:2048],
                                                        psB[:, 1024:2048]), [(S_["evat"], 1)])
        r.wait("vector", S_["d"], h + 1)
        r.wait("vector", S_["alnc"], 16)
        r.op("vector", lambda eng: eng.reciprocal(obuf[1][:, :], psB[:, :]),
             [(S_["rec"], 1)])
        r.wait("vector", S_["rec"], S_["rec"].n)
        r.op("vector", lambda eng, h=h: eng.tensor_mul(attnT(h), attnT(h), obuf[1][:, :]),
             [(S_["nrm"], 1)])
    NRM_TOTAL = S_["nrm"].n

    psB_cm.__exit__(None, None, None)
    psB2 = e(nc.psum_tensor("psB2", [128, 1024], F32))
    psT = e(nc.psum_tensor("psT", [128, 2048], BF16))

    slot_items = []
    for g in range(4):
        slot_items.append(("wo", g))
        slot_items.append(("xr", g))
    slot_of = {}
    slot_uses = [0, 0, 0]
    for i, item in enumerate(slot_items):
        slot_of[item] = i % 3

    def item_consumed(item):
        knd, g = item
        return (S_["op"], 16 * (g + 1)) if knd == "wo" else (S_["r1"], 16 * (g + 1))

    for i, (knd, g) in enumerate(slot_items):
        slot = i % 3
        if slot_uses[slot] == 0:
            r.wait("gpsimd", S_["vmm"], 128)
        else:
            sem, v = item_consumed(slot_items[i - 3])
            r.wait("gpsimd", sem, v)
        src = woT if knd == "wo" else xnat
        r.op("gpsimd", lambda eng, slot=slot, g=g, src=src: eng.dma_start(
            C_[slot][:, :], dap(src, g * 512, [[D, 128], [128 * D, 16], [1, 512]])),
            [(S_["c" + str(slot)], 16)])
        slot_uses[slot] += 1
    c_counts = {}
    cnt = [0, 0, 0]
    for i, item in enumerate(slot_items):
        slot = i % 3
        cnt[slot] += 16
        c_counts[item] = cnt[slot]

    r.wait("tensor", S_["nrm"], NRM_TOTAL)
    opidx = 0
    for g in range(4):
        swo = slot_of[("wo", g)]
        sxr = slot_of[("xr", g)]
        r.wait("tensor", S_["c" + str(swo)], c_counts[("wo", g)])
        for qt in range(16):
            half = opidx % 2
            r.wait("tensor", S_["r1"], opidx - 1)
            for dc in range(16):
                incs = [(S_["op"], 1)] if dc == 15 else []
                r.op("tensor", lambda eng, g=g, qt=qt, dc=dc, half=half, swo=swo:
                     eng.matmul(psB2[:, half * 512:(half + 1) * 512],
                                B_[dc // 4][:, (dc % 4) * 2048 + qt * 128:(dc % 4) * 2048 + (qt + 1) * 128],
                                C_[swo][:, dc * 512:(dc + 1) * 512],
                                start=(dc == 0), stop=(dc == 15)), incs)
            r.wait("vector", S_["op"], opidx + 1)
            if qt == 0:
                r.wait("vector", S_["c" + str(sxr)], c_counts[("xr", g)])
            r.op("vector", lambda eng, g=g, qt=qt, half=half, sxr=sxr:
                 eng.tensor_add(A_[qt][:, g * 512:(g + 1) * 512],
                                psB2[:, half * 512:(half + 1) * 512],
                                C_[sxr][:, qt * 512:(qt + 1) * 512]), [(S_["r1"], 1)])
            opidx += 1

    for qt in range(16):
        r.wait("vector", S_["r1"], 48 + qt + 1)
        r.op("vector", lambda eng, qt=qt: eng.reduce_sum(stcol[:, qt:qt + 1],
                                                         A_[qt][:, :], axis=AX.X),
             [(S_["st1"], 1)])
        r.wait("scalar", S_["r1"], 48 + qt + 1)
        r.wait("scalar", S_["sq1"], S_["sq1"].n)
        r.op("scalar", lambda eng, qt=qt: eng.activation(obuf[1][:, :], A_[qt][:, :],
                                                         AF.Square,
                                                         accum_out=stcol[:, 16 + qt:17 + qt]),
             [(S_["sq1"], 1)])

    def ln_finalize(r_s1, r_s2, r_a, r_b):
        def rng(base):
            return stcol[:, base:base + 16]
        r.op("vector", lambda eng: eng.tensor_scalar(rng(r_a), rng(r_s1),
                                                     float(1.0 / D), None, ALU.mult),
             [(S_["vch"], 1)])
        r.op("vector", lambda eng: eng.tensor_scalar(rng(r_b), rng(r_s2),
                                                     float(1.0 / D), None, ALU.mult),
             [(S_["vch"], 1)])
        r.wait("vector", S_["vch"], S_["vch"].n)
        r.op("vector", lambda eng: eng.tensor_mul(rng(r_s1), rng(r_a), rng(r_a)),
             [(S_["vch"], 1)])
        r.wait("vector", S_["vch"], S_["vch"].n)
        r.op("vector", lambda eng: eng.tensor_sub(rng(r_s2), rng(r_b), rng(r_s1)),
             [(S_["vch"], 1)])
        r.wait("scalar", S_["vch"], S_["vch"].n)
        r.op("scalar", lambda eng: eng.activation(rng(r_b), rng(r_s2), AF.Sqrt,
                                                  bias=np.float32(EPS)), [(S_["sch"], 1)])
        r.wait("vector", S_["sch"], S_["sch"].n)
        r.op("vector", lambda eng: eng.reciprocal(rng(r_s1), rng(r_b)), [(S_["vch"], 1)])
        r.wait("vector", S_["vch"], S_["vch"].n)
        r.op("vector", lambda eng: eng.tensor_mul(rng(r_b), rng(r_a), rng(r_s1)),
             [(S_["vch"], 1)])
        r.wait("vector", S_["vch"], S_["vch"].n)
        r.op("vector", lambda eng: eng.tensor_scalar(rng(r_s2), rng(r_b),
                                                     -1.0, None, ALU.mult),
             [(S_["vch"], 1)])

    r.wait("vector", S_["st1"], 16)
    r.wait("vector", S_["sq1"], 16)
    ln_finalize(0, 16, 32, 48)
    LN1_FIN = S_["vch"].n

    sem, v = item_consumed(("wo", 3))
    r.wait("gpsimd", sem, v)
    r.op("gpsimd", lambda eng: eng.dma_start(
        C_[0][:, :], dap(ATd, 0, [[NE * RK, 128], [128 * NE * RK, 16], [1, 512]])),
        [(S_["wF"], 16)])
    sem, v = item_consumed(("xr", 3))
    r.wait("gpsimd", sem, v)
    r.op("gpsimd", lambda eng: eng.dma_start(
        C_[1][:, :], dap(BTd, 0, [[D, 128], [128 * D, 4], [1, 2048]])), [(S_["bc"], 16)])

    ps_t_last = [None, None]
    mh_txes_after = {}
    tx_i = 0
    e8_count_at = {}

    def x1T_block(mc, qt):
        base = (mc % 4) * 2048 + qt * 128
        return B_[mc // 4][:, base:base + 128]

    for qt in range(16):
        r.wait("vector", S_["vch"], LN1_FIN)
        r.op("vector", lambda eng, qt=qt: eng.tensor_scalar(
            A_[qt][:, :], A_[qt][:, :], stcol[:, qt:qt + 1], stcol[:, 16 + qt:17 + qt],
            ALU.mult, ALU.add), [(S_["n1"], 1)])

        r.wait("tensor", S_["n1"], qt + 1)
        for mc in range(16):
            bank = tx_i % 2
            if ps_t_last[bank] is not None:
                sem, cntv = ps_t_last[bank]
                r.wait("tensor", sem, cntv)
            r.op("tensor", lambda eng, qt=qt, mc=mc, bank=bank:
                 eng.transpose(psT[:, bank * 1024:bank * 1024 + 128],
                               A_[qt][:, mc * 128:(mc + 1) * 128], identb[:, :]),
                 [(S_["tx"], 1)])
            r.wait("vector", S_["tx"], S_["tx"].n)
            r.op("vector", lambda eng, qt=qt, mc=mc, bank=bank:
                 eng.tensor_copy(x1T_block(mc, qt), psT[:, bank * 1024:bank * 1024 + 128]),
                 [(S_["txev"], 1)])
            ps_t_last[bank] = (S_["txev"], S_["txev"].n)
            tx_i += 1

        r.wait("tensor", S_["txev"], S_["txev"].n)
        if qt == 0:
            r.wait("tensor", S_["wF"], 16)
            r.wait("tensor", S_["bc"], 16)
            r.wait("tensor", S_["ld0g"], 16)
        hhalf = qt % 2
        r.wait("tensor", S_["mk"], qt - 1)
        r.wait("tensor", S_["lg"], qt - 1)
        for mc in range(16):
            r.op("tensor", lambda eng, qt=qt, mc=mc, hhalf=hhalf:
                 eng.matmul(psB2[:, hhalf * 512:(hhalf + 1) * 512],
                            x1T_block(mc, qt), C_[0][:, mc * 512:(mc + 1) * 512],
                            start=(mc == 0), stop=(mc == 15)), [])
            r.op("tensor", lambda eng, qt=qt, mc=mc, hhalf=hhalf:
                 eng.matmul(psA[:, hhalf * 512:hhalf * 512 + 8],
                            x1T_block(mc, qt), wrtb[:, mc * 8:(mc + 1) * 8],
                            start=(mc == 0), stop=(mc == 15)),
                 [(S_["h"], 1)] if mc == 15 else [])

        r.wait("vector", S_["h"], qt + 1)
        if qt == 0:
            r.wait("vector", S_["ld0"], lds0)
        if qt >= 2:
            r.wait("vector", S_["sch"], e8_count_at[qt - 2])
        lg8 = lgts[:, (qt % 2) * 8:(qt % 2) * 8 + 8]
        r.op("vector", lambda eng, qt=qt, hhalf=hhalf, lg8=lg8:
             eng.tensor_add(lg8, psA[:, hhalf * 512:hhalf * 512 + 8], brs[:, :]),
             [(S_["lg"], 1)])
        if qt >= 1:
            r.wait("vector", S_["sch"], e8_count_at[qt - 1])
        r.wait("vector", S_["lg"], S_["lg"].n)
        r.op("vector", lambda eng, lg8=lg8: eng.reduce_max(smf[:, 16:17], lg8, axis=AX.X,
                                                           negate=True), [(S_["vch"], 1)])
        r.op("vector", lambda eng, lg8=lg8: eng.reduce_max(smf[:, 17:18], lg8, axis=AX.X),
             [(S_["vch"], 1)])
        r.wait("scalar", S_["vch"], S_["vch"].n)
        r.op("scalar", lambda eng, lg8=lg8: eng.activation(smf[:, 0:8], lg8, AF.Exp,
                                                           bias=smf[:, 16:17],
                                                           accum_out=smf[:, 18:19]),
             [(S_["sch"], 1)])
        e8_count_at[qt] = S_["sch"].n
        r.wait("vector", S_["sch"], S_["sch"].n)
        r.op("vector", lambda eng: eng.reciprocal(smf[:, 19:20], smf[:, 18:19]),
             [(S_["vch"], 1)])
        r.op("vector", lambda eng, lg8=lg8: eng.tensor_scalar(smf[:, 8:16], lg8,
                                                              smf[:, 17:18], None, ALU.is_ge),
             [(S_["vch"], 1)])
        r.wait("vector", S_["vch"], S_["vch"].n)
        r.op("vector", lambda eng, qt=qt: eng.tensor_scalar(coefT[:, qt * 8:(qt + 1) * 8],
                                                            smf[:, 8:16], smf[:, 19:20],
                                                            None, ALU.mult), [(S_["co"], 1)])

        r.wait("vector", S_["co"], qt + 1)
        if qt >= 2:
            r.wait("vector", S_["txes"], mh_txes_after[qt - 2])
        for ex in range(NE):
            r.op("vector", lambda eng, qt=qt, ex=ex, hhalf=hhalf:
                 eng.tensor_scalar(mh[:, hhalf * 512 + ex * 64: hhalf * 512 + (ex + 1) * 64],
                                   psB2[:, hhalf * 512 + ex * 64: hhalf * 512 + (ex + 1) * 64],
                                   coefT[:, qt * 8 + ex:qt * 8 + ex + 1], None, ALU.mult),
                 [(S_["mk"], 1)] if ex == NE - 1 else [])

        r.wait("tensor", S_["mk"], qt + 1)
        for blk in range(4):
            bank = tx_i % 2
            if ps_t_last[bank] is not None:
                sem, cntv = ps_t_last[bank]
                r.wait("tensor", sem, cntv)
            r.op("tensor", lambda eng, qt=qt, blk=blk, bank=bank, hhalf=hhalf:
                 eng.transpose(psT[:, bank * 1024:bank * 1024 + 128],
                               mh[:, hhalf * 512 + blk * 128: hhalf * 512 + (blk + 1) * 128],
                               identb[:, :]), [(S_["tx"], 1)])
            if qt == 0 and blk == 0:
                r.wait("scalar", S_["r1"], 48)
            r.wait("scalar", S_["tx"], S_["tx"].n)
            r.op("scalar", lambda eng, qt=qt, blk=blk, bank=bank:
                 eng.copy(C_[2][:, blk * 2048 + qt * 128: blk * 2048 + (qt + 1) * 128],
                          psT[:, bank * 1024:bank * 1024 + 128]), [(S_["txes"], 1)])
            ps_t_last[bank] = (S_["txes"], S_["txes"].n)
            tx_i += 1
        mh_txes_after[qt] = S_["txes"].n

        r.wait("tensor", S_["txes"], S_["txes"].n)
        for oc in range(4):
            yidx = qt * 4 + oc
            ybank = 2 + yidx % 2
            r.wait("tensor", S_["x2"], yidx - 1)
            for pair in range(4):
                r.op("tensor", lambda eng, qt=qt, oc=oc, pair=pair, ybank=ybank:
                     eng.matmul(psA[:, ybank * 512:(ybank + 1) * 512],
                                C_[2][:, pair * 2048 + qt * 128: pair * 2048 + (qt + 1) * 128],
                                C_[1][:, pair * 2048 + oc * 512: pair * 2048 + (oc + 1) * 512],
                                start=(pair == 0), stop=(pair == 3)),
                     [(S_["yy"], 1)] if pair == 3 else [])
            r.wait("vector", S_["yy"], yidx + 1)
            r.op("vector", lambda eng, qt=qt, oc=oc, ybank=ybank:
                 eng.scalar_tensor_tensor(A_[qt][:, oc * 512:(oc + 1) * 512],
                                          psA[:, ybank * 512:(ybank + 1) * 512],
                                          1.0 / 64.0, A_[qt][:, oc * 512:(oc + 1) * 512],
                                          ALU.mult, ALU.add), [(S_["x2"], 1)])
        r.wait("vector", S_["x2"], 4 * (qt + 1))
        r.op("vector", lambda eng, qt=qt: eng.reduce_sum(stcol[:, 32 + qt:33 + qt],
                                                         A_[qt][:, :], axis=AX.X),
             [(S_["st1"], 1)])
        r.wait("scalar", S_["x2"], 4 * (qt + 1))
        r.wait("scalar", S_["sq1"], S_["sq1"].n)
        r.op("scalar", lambda eng, qt=qt: eng.activation(obuf[1][:, :], A_[qt][:, :],
                                                         AF.Square,
                                                         accum_out=stcol[:, 48 + qt:49 + qt]),
             [(S_["sq1"], 1)])

    r.wait("vector", S_["st1"], 32)
    r.wait("vector", S_["sq1"], 32)
    ln_finalize(32, 48, 0, 16)
    LN2_FIN = S_["vch"].n

    for qt in range(16):
        r.wait("vector", S_["vch"], LN2_FIN)
        r.wait("vector", S_["out"], 16 * (qt - 1))
        r.op("vector", lambda eng, qt=qt: eng.tensor_scalar(
            obuf[qt % 2][:, :], A_[qt][:, :], stcol[:, 32 + qt:33 + qt],
            stcol[:, 48 + qt:49 + qt], ALU.mult, ALU.add), [(S_["n2"], 1)])
        r.wait("sync", S_["n2"], qt + 1)
        r.wait("sync", S_["out"], 16 * qt)
        r.op("sync", lambda eng, qt=qt: eng.dma_start(y.ap()[qt * 128:(qt + 1) * 128, :],
                                                      obuf[qt % 2][:, :]), [(S_["out"], 16)])
    r.wait("sync", S_["out"], 16 * 16)

    r.check()

    with nc.Block() as block:
        @block.tensor
        def _(eng):
            for f in r.ops["tensor"]:
                f(eng)

        @block.scalar
        def _(eng):
            for f in r.ops["scalar"]:
                f(eng)

        @block.vector
        def _(eng):
            for f in r.ops["vector"]:
                f(eng)

        @block.sync
        def _(eng):
            for f in r.ops["sync"]:
                f(eng)

        @block.gpsimd
        def _(eng):
            for f in r.ops["gpsimd"]:
                f(eng)

    ctx.close()
    return nc


_NC_CACHE = None


def _get_nc():
    global _NC_CACHE
    if _NC_CACHE is None:
        _NC_CACHE = build_nc()
    return _NC_CACHE


def make_in_maps(L0_attn_x, L0_attn_wq, L0_attn_wk, L0_attn_wv, L0_attn_wo,
                 L0_ffn_A, L0_ffn_B, L0_ffn_Wr, L0_ffn_br):
    x = np.asarray(L0_attn_x, dtype=np.float32)
    shared = {
        "wqT": np.ascontiguousarray(np.asarray(L0_attn_wq, np.float32).T),
        "wkT": np.ascontiguousarray(np.asarray(L0_attn_wk, np.float32).T),
        "wvT": np.ascontiguousarray(np.asarray(L0_attn_wv, np.float32).T),
        "woT": np.ascontiguousarray(np.asarray(L0_attn_wo, np.float32).T),
        "ATd": np.ascontiguousarray(
            np.asarray(L0_ffn_A, np.float32).transpose(2, 0, 1).reshape(D, NE * RK)),
        "BTd": np.ascontiguousarray(
            np.asarray(L0_ffn_B, np.float32).transpose(0, 2, 1).reshape(4 * 128, D)),
        "WrTd": np.ascontiguousarray(np.asarray(L0_ffn_Wr, np.float32).T),
        "brb": np.ascontiguousarray(
            np.broadcast_to(np.asarray(L0_ffn_br, np.float32).reshape(1, NE), (128, NE))),
        "identd": np.eye(128, dtype=ml_dtypes.bfloat16),
        "onesmd": np.ones((128, 128), dtype=np.float32),
        "onesrd": np.ones((1, 128), dtype=ml_dtypes.bfloat16),
        "onesmbd": np.ones((128, 128), dtype=ml_dtypes.bfloat16),
    }
    in_maps = []
    for c in range(8):
        b, hf = c // 2, c % 2
        own = x[b, hf * NQ:(hf + 1) * NQ, :]
        oth = x[b, (1 - hf) * NQ:(2 - hf) * NQ, :]
        m = dict(shared)
        m["xTb"] = np.ascontiguousarray(np.concatenate([own.T, oth.T], axis=1))
        m["xnat"] = np.ascontiguousarray(own)
        in_maps.append(m)
    return in_maps


def kernel(**inputs):
    in_maps = make_in_maps(**inputs)
    nc = _get_nc()
    res = run_bass_kernel_spmd(nc, in_maps, core_ids=list(range(8)))
    out = np.empty((4, 4096, 2048), dtype=np.float32)
    for c in range(8):
        b, hf = c // 2, c % 2
        out[b, hf * NQ:(hf + 1) * NQ, :] = res.results[c]["y"]
    return out


# version 05
# speedup vs baseline: 1.0677x; 1.0677x over previous
import numpy as np
import ml_dtypes
import concourse.bass as bass
import concourse.mybir as mybir
from concourse.bass_utils import run_bass_kernel_spmd

F32 = mybir.dt.float32
BF16 = mybir.dt.bfloat16
AF = mybir.ActivationFunctionType
ALU = mybir.AluOpType
AX = mybir.AxisListType

D = 2048
S = 4096
NQ = 2048
H = 16
HD = 128
NE = 8
RK = 64
BETA = 9.5
EPS = 1e-5

ENGS = ("tensor", "scalar", "vector", "sync", "gpsimd")


class Sem:
    def __init__(self, h, name):
        self.h = h
        self.name = name
        self.n = 0


class Rec:
    def __init__(self):
        self.ops = {e: [] for e in ENGS}
        self.waits = []

    def wait(self, e, sem, val):
        if val <= 0:
            return
        self.waits.append((sem, val))
        self.ops[e].append(lambda eng, s=sem, v=val: eng.wait_ge(s.h, v))

    def op(self, e, build, incs=()):
        incs = tuple(incs)
        for s, a in incs:
            s.n += a
        assert len(incs) <= 1, "one sem update per instruction"
        attach, spill = incs[:1], incs[1:]

        def f(eng, build=build, attach=attach, spill=spill):
            inst = build(eng)
            for s, a in attach:
                inst = inst.then_inc(s.h, a)
            for s, a in spill:
                eng.sem_inc(s.h, a)
        self.ops[e].append(f)

    def check(self):
        for s, v in self.waits:
            assert v <= s.n, f"wait on {s.name} for {v} but total incs {s.n}"


def dap(t, off, dims):
    return bass.AP(t, off, [list(d) for d in dims])


def build_nc():
    nc = bass.Bass("TRN2", target_bir_lowering=False)

    xTb = nc.dram_tensor("xTb", [D, S], F32, kind="ExternalInput")
    xnat = nc.dram_tensor("xnat", [NQ, D], F32, kind="ExternalInput")
    wqT = nc.dram_tensor("wqT", [D, D], F32, kind="ExternalInput")
    wkT = nc.dram_tensor("wkT", [D, D], F32, kind="ExternalInput")
    wvT = nc.dram_tensor("wvT", [D, D], F32, kind="ExternalInput")
    woT = nc.dram_tensor("woT", [D, D], F32, kind="ExternalInput")
    ATd = nc.dram_tensor("ATd", [D, NE * RK], F32, kind="ExternalInput")
    BTd = nc.dram_tensor("BTd", [4 * 128, D], F32, kind="ExternalInput")
    WrTd = nc.dram_tensor("WrTd", [D, NE], F32, kind="ExternalInput")
    brb = nc.dram_tensor("brb", [128, NE], F32, kind="ExternalInput")
    identd = nc.dram_tensor("identd", [128, 128], BF16, kind="ExternalInput")
    onesmd = nc.dram_tensor("onesmd", [128, 128], F32, kind="ExternalInput")
    onesmbd = nc.dram_tensor("onesmbd", [128, 128], BF16, kind="ExternalInput")
    onesrd = nc.dram_tensor("onesrd", [1, 128], BF16, kind="ExternalInput")

    y = nc.dram_tensor("y", [NQ, D], F32, kind="ExternalOutput")

    kdram = nc.dram_tensor("kdram", [8, 128, 8192], BF16)
    vdram = nc.dram_tensor("vdram", [S, D], BF16)
    qdram = nc.dram_tensor("qdram", [4, 128, 8192], BF16)
    negdram = nc.dram_tensor("negdram", [16, 2048], BF16)

    from contextlib import ExitStack
    ctx = ExitStack()
    e = ctx.enter_context

    AB = [e(nc.sbuf_tensor(f"AB{i}", [128, 8192], BF16)) for i in range(4)]
    A_ = [AB[i // 4][:, (i % 4) * 2048:(i % 4 + 1) * 2048] for i in range(16)]
    B_ = [e(nc.sbuf_tensor(f"B{i}", [128, 8192], BF16)) for i in range(4)]
    C_ = [e(nc.sbuf_tensor(f"C{i}", [128, 8192], BF16)) for i in range(3)]
    obuf = [e(nc.sbuf_tensor(f"obuf{i}", [128, 2048], F32)) for i in range(2)]
    mh = e(nc.sbuf_tensor("mh", [128, 1024], BF16))
    coefT = e(nc.sbuf_tensor("coefT", [128, 128], F32))
    stcol = e(nc.sbuf_tensor("stcol", [128, 64], F32))
    smf = e(nc.sbuf_tensor("smf", [128, 32], F32))
    lgts = e(nc.sbuf_tensor("lgts", [128, 16], F32))
    wrtb = e(nc.sbuf_tensor("wrtb", [128, 128], BF16))
    brs = e(nc.sbuf_tensor("brs", [128, NE], F32))
    identb = e(nc.sbuf_tensor("identb", [128, 128], BF16))
    onesm = e(nc.sbuf_tensor("onesm", [128, 128], F32))
    onesr = e(nc.sbuf_tensor("onesr", [1, 128], BF16))
    onesmb = e(nc.sbuf_tensor("onesmb", [128, 128], BF16))
    acc2 = e(nc.sbuf_tensor("acc2", [128, 2048], BF16))
    pa = e(nc.sbuf_tensor("pa", [128, 32], F32))
    X = e(nc.sbuf_tensor("X", [128, 2048], BF16))

    psA = e(nc.psum_tensor("psA", [128, 1024], F32))
    psB_cm = nc.psum_tensor("psB", [128, 1024], F32)
    psB = psB_cm.__enter__()
    psC_cm = nc.psum_tensor("psC", [128, 1024], F32)
    psC = psC_cm.__enter__()
    psF_cm = nc.psum_tensor("psF", [128, 1024], F32)
    psF = psF_cm.__enter__()

    semnames = ("wA wAf x0 x1 xf0 xf1 mmP1 evP1 sp kv0 kv1 kvg0 kvg1 sc exp avf "
                "bc nrm c0 c1 c2 op r1 st1 sq1 vch sch n1 tx "
                "asc ard stq xrg xr csd dnb rcp "
                "txev txes wF h lg co mk yy x2 n2 out ld0 ld0g").split()
    S_ = {n: Sem(e(nc.semaphore(f"s_{n}")), n) for n in semnames}

    r = Rec()

    for dst, src in ((identb, identd), (onesm, onesmd), (onesmb, onesmbd), (onesr, onesrd), (brs, brb)):
        r.wait("sync", S_["ld0"], S_["ld0"].n)
        r.op("sync", lambda eng, d=dst, s=src: eng.dma_start(d[:, :], s.ap()[:, :]),
             [(S_["ld0"], 16)])
    r.op("gpsimd", lambda eng: eng.dma_start(
        wrtb[:, :], dap(WrTd, 0, [[NE, 128], [128 * NE, 16], [1, NE]])), [(S_["ld0g"], 16)])
    lds0 = S_["ld0"].n

    def pool6(i):
        i = i % 6
        if i < 2:
            return psB[:, i * 512:(i + 1) * 512]
        if i < 4:
            return psC[:, (i - 2) * 512:(i - 2 + 1) * 512]
        return psF[:, (i - 4) * 512:(i - 4 + 1) * 512]

    g_grp = 0
    n_spills = 0
    vslot_i = 0
    stage_last_spill = {}
    xcnk_free_at = {}
    pass_end_mm = []

    passes = (("k", wkT, 8, kdram), ("v", wvT, 8, vdram), ("q", wqT, 4, qdram))
    for p_i, (kind, wT, nchunks, dest) in enumerate(passes):
        for g4 in range(4):
            if g4 == 0 and p_i > 0:
                r.wait("gpsimd", S_["mmP1"], pass_end_mm[p_i - 1])
            wsem = S_["wA"] if g4 % 2 == 0 else S_["wAf"]
            r.wait("gpsimd", wsem, wsem.n)
            r.op("gpsimd", lambda eng, g4=g4, wT=wT: eng.dma_start(
                AB[g4][:, :],
                dap(wT, g4 * 4 * 128 * D, [[D, 128], [128 * D, 4], [1, D]])),
                [(wsem, 16)])
        wA_after = (S_["wA"].n, S_["wAf"].n)

        for t in range(nchunks):
            par = t % 2
            if par in xcnk_free_at:
                r.wait("gpsimd", S_["mmP1"], xcnk_free_at[par])
            r.op("gpsimd", lambda eng, par=par, t=t: eng.dma_start(
                C_[par][:, :], dap(xTb, t * 512, [[S, 128], [128 * S, 16], [1, 512]])),
                [(S_["x" + str(par)], 16)])
            x_after = S_["x" + str(par)].n

            r.wait("tensor", S_["wA"], wA_after[0])
            r.wait("tensor", S_["wAf"], wA_after[1])
            r.wait("tensor", S_["x" + str(par)], x_after)

            if kind in ("k", "q"):
                for dk in range(16):
                    slot = g_grp % 6
                    r.wait("tensor", S_["evP1"], g_grp - 5)
                    for mc in range(16):
                        incs = [(S_["mmP1"], 1)] if mc == 15 else []
                        r.op("tensor", lambda eng, slot=slot, mc=mc, dk=dk, par=par:
                             eng.matmul(pool6(slot),
                                        A_[mc][:, dk * 128:(dk + 1) * 128],
                                        C_[par][:, mc * 512:(mc + 1) * 512],
                                        start=(mc == 0), stop=(mc == 15)), incs)
                    r.wait("scalar", S_["mmP1"], S_["mmP1"].n)
                    if dk == 0 and ("kq", par) in stage_last_spill:
                        r.wait("scalar", S_["sp"], 16 * stage_last_spill[("kq", par)])
                    r.op("scalar", lambda eng, slot=slot, dk=dk, par=par:
                         eng.copy(B_[par][:, dk * 512:(dk + 1) * 512], pool6(slot)),
                         [(S_["evP1"], 1)])
                    g_grp += 1
                r.wait("sync", S_["evP1"], S_["evP1"].n)
                r.wait("sync", S_["sp"], S_["sp"].n)
                r.op("sync", lambda eng, t=t, par=par, dest=dest:
                     eng.dma_start(dest.ap()[t, :, :], B_[par][:, :]), [(S_["sp"], 16)])
                n_spills += 1
                stage_last_spill[("kq", par)] = n_spills
            else:
                for tt in range(4):
                    vs = vslot_i % 4
                    tt_global = t * 4 + tt
                    for dvc in range(4):
                        slot = g_grp % 6
                        r.wait("tensor", S_["evP1"], g_grp - 5)
                        for mc in range(16):
                            incs = [(S_["mmP1"], 1)] if mc == 15 else []
                            r.op("tensor", lambda eng, slot=slot, mc=mc, tt=tt, dvc=dvc, par=par:
                                 eng.matmul(pool6(slot),
                                            C_[par][:, mc * 512 + tt * 128: mc * 512 + (tt + 1) * 128],
                                            A_[mc][:, dvc * 512:(dvc + 1) * 512],
                                            start=(mc == 0), stop=(mc == 15)), incs)
                        r.wait("scalar", S_["mmP1"], S_["mmP1"].n)
                        if dvc == 0 and ("v", vs) in stage_last_spill:
                            r.wait("scalar", S_["sp"], 16 * stage_last_spill[("v", vs)])
                        r.op("scalar", lambda eng, slot=slot, vs=vs, dvc=dvc:
                             eng.copy(B_[2][:, vs * 2048 + dvc * 512: vs * 2048 + (dvc + 1) * 512],
                                      pool6(slot)), [(S_["evP1"], 1)])
                        g_grp += 1
                    r.wait("sync", S_["evP1"], S_["evP1"].n)
                    r.wait("sync", S_["sp"], S_["sp"].n)
                    r.op("sync", lambda eng, tt_global=tt_global, vs=vs:
                         eng.dma_start(vdram.ap()[tt_global * 128:(tt_global + 1) * 128, :],
                                       B_[2][:, vs * 2048:(vs + 1) * 2048]), [(S_["sp"], 16)])
                    n_spills += 1
                    stage_last_spill[("v", vs)] = n_spills
                    vslot_i += 1
            xcnk_free_at[par] = S_["mmP1"].n
        pass_end_mm.append(S_["mmP1"].n)

    P1_SPILLS = S_["sp"].n
    P1_MM = S_["mmP1"].n
    EV_TOT = S_["evP1"].n

    def ktbuf(par, piece):
        return A_[2 * par + piece]

    def vbuf(par, piece):
        return A_[4 + 2 * par + piece]

    def qbuf(par):
        return A_[8 + par]

    def ring(i):
        return A_[10 + (i % 8) // 2][:, ((i % 8) % 2) * 1024:(((i % 8) % 2) + 1) * 1024]

    def attnT(vh, lo, width):
        hh, qh = vh // 2, vh % 2
        base = (hh % 4) * 2048 + qh * 1024 + lo
        return B_[hh // 4][:, base:base + width]

    def chslot(vh):
        return acc2[:, (vh % 2) * 1024:(vh % 2 + 1) * 1024]

    def pslot(vh):
        return acc2[:, ((vh + 1) % 2) * 1024:((vh + 1) % 2 + 1) * 1024]

    NVH = 32
    kv_snap = {}
    kvg_snap = {}

    def emit_loads(hh):
        par = hh % 2
        if hh < 2:
            r.wait("sync", S_["sp"], P1_SPILLS)
            r.wait("sync", S_["mmP1"], P1_MM)
            r.wait("gpsimd", S_["sp"], P1_SPILLS)
        else:
            r.wait("sync", S_["sc"], 64 * (hh - 1))
            r.wait("gpsimd", S_["avf"], 64 * (hh - 1))
        ks = S_["kv" + str(par)]
        gs = S_["kvg" + str(par)]
        for piece in range(2):
            r.wait("sync", ks, ks.n)
            r.op("sync", lambda eng, hh=hh, g=piece, par=par: eng.dma_start(
                ktbuf(par, g)[:, :],
                dap(kdram, (4 * g) * 128 * 8192 + hh * 512,
                    [[8192, 128], [128 * 8192, 4], [1, 512]])), [(ks, 16)])
            r.wait("gpsimd", gs, gs.n)
            r.op("gpsimd", lambda eng, hh=hh, g=piece, par=par: eng.dma_start(
                vbuf(par, g)[:, :],
                dap(vdram, (g * 16 * 128) * D + hh * 128,
                    [[D, 128], [128 * D, 16], [1, 128]])), [(gs, 16)])
        r.wait("sync", ks, ks.n)
        r.op("sync", lambda eng, hh=hh, par=par: eng.dma_start(
            qbuf(par)[:, :],
            dap(qdram, hh * 512, [[8192, 128], [128 * 8192, 4], [1, 512]])),
            [(ks, 16)])
        kv_snap[hh] = ks.n
        kvg_snap[hh] = gs.n

    turns = []
    turn_of_kt = {}
    turn_of_den = {}
    for vh in range(NVH):
        for kt in range(32):
            if kt == 2 and vh >= 1:
                turn_of_den[vh - 1] = len(turns)
                turns.append(("den", vh - 1))
            turn_of_kt[(vh, kt)] = len(turns)
            turns.append(("kt", vh, kt))
    turn_of_den[31] = len(turns)
    turns.append(("den", 31))

    def turn_slot(T):
        return psA if T % 2 == 0 else psC

    def turn_consumed(T):
        t = turns[T]
        if t[0] == "kt":
            return (S_["exp"], t[1] * 32 + t[2] + 1)
        return (S_["rcp"], 2 * (t[1] + 1))

    def emit_passA_mm(ga):
        vh, a = ga // 64, ga % 64
        hh, qh, par = vh // 2, vh % 2, (vh // 2) % 2
        j, c = a // 8, a % 8
        if a == 0:
            r.wait("tensor", S_["kv" + str(par)], kv_snap[hh])
            r.wait("tensor", S_["kvg" + str(par)], kvg_snap[hh])
        if ga == 0:
            r.wait("tensor", S_["evP1"], EV_TOT)
        r.wait("tensor", S_["ard"], ga - 1)
        r.op("tensor", lambda eng, ga=ga, qh=qh, par=par, j=j, c=c: eng.matmul(
            psF[:, (ga % 2) * 512:(ga % 2 + 1) * 512],
            qbuf(par)[:, qh * 1024 + j * 128:qh * 1024 + (j + 1) * 128],
            ktbuf(par, c // 4)[:, (c % 4) * 512:(c % 4 + 1) * 512],
            start=True, stop=True), [(S_["asc"], 1)])

    def emit_passA_dve(ga):
        vh, a = ga // 64, ga % 64
        j, c = a // 8, a % 8
        scc = (j % 2) * 8 + c
        r.wait("vector", S_["asc"], ga + 1)
        r.op("vector", lambda eng, ga=ga, scc=scc: eng.reduce_max(
            pa[:, scc:scc + 1], psF[:, (ga % 2) * 512:(ga % 2 + 1) * 512],
            axis=AX.X), [(S_["ard"], 1)])
        if c == 7:
            stc = 16 + (vh % 2) * 8 + j
            r.wait("vector", S_["ard"], S_["ard"].n)
            if vh >= 2:
                r.wait("vector", S_["xrg"], 16 * (8 * (vh - 2) + j + 1))
            r.op("vector", lambda eng, j=j, stc=stc: eng.reduce_max(
                pa[:, stc:stc + 1], pa[:, (j % 2) * 8:(j % 2) * 8 + 8],
                axis=AX.X, negate=True), [(S_["stq"], 1)])

    def emit_stat_dma(vh, j):
        stc = 16 + (vh % 2) * 8 + j
        r.wait("gpsimd", S_["xrg"], S_["xrg"].n)
        r.wait("gpsimd", S_["stq"], 8 * vh + j + 1)
        r.op("gpsimd", lambda eng, vh=vh, j=j, stc=stc: eng.dma_start(
            dap(negdram, vh * 1024 + j * 128, [[1, 128]]),
            pa[:, stc:stc + 1]), [(S_["xrg"], 16)])

    def emit_xrow(vh):
        qh = vh % 2
        r.wait("sync", S_["xr"], S_["xr"].n)
        r.wait("sync", S_["xrg"], 16 * 8 * (vh + 1))
        if vh >= 2:
            r.wait("sync", S_["sc"], 32 * (vh - 1))
        r.op("sync", lambda eng, vh=vh, qh=qh: eng.dma_start(
            X[32 * qh:32 * qh + 1, qh * 1024:(qh + 1) * 1024],
            dap(negdram, vh * 1024, [[1024, 1], [1, 1024]])), [(S_["xr"], 16)])

    def casc_ops(vh, t):
        CH, P = (lambda: chslot(vh)), (lambda: pslot(vh))
        R = lambda k: (lambda: ring(32 * vh + k))
        if t == 3:
            return [(CH, R(0), R(1), [0, 1])]
        if 4 <= t <= 9:
            return [(CH, CH, R(t - 2), [t - 2])]
        if t == 10:
            return [(P, R(8), R(9), [8, 9])]
        if 11 <= t <= 16:
            return [(P, P, R(t - 1), [t - 1])]
        if t == 17:
            return [(P, P, CH, [])]
        if t == 18:
            return [(CH, R(16), R(17), [16, 17])]
        if 19 <= t <= 24:
            return [(CH, CH, R(t - 1), [t - 1])]
        if t == 25:
            return [(P, P, CH, [])]
        if t == 26:
            return [(CH, R(24), R(25), [24, 25])]
        if 27 <= t <= 31:
            return [(CH, CH, R(t - 1), [t - 1])]
        if t == 32:
            return [(CH, CH, R(31), [31])]
        if t == 33:
            return [(P, P, CH, [])]
        return []

    csd_at = {}
    csd_final = {}

    def emit_cascade(vh, t):
        for dst, s0, s1, cons in casc_ops(vh, t):
            if cons:
                r.wait("vector", S_["exp"], 32 * vh + max(cons) + 1)
            if t == 3:
                r.wait("vector", S_["dnb"], 2 * vh)
            r.wait("vector", S_["csd"], S_["csd"].n)
            r.op("vector", lambda eng, dst=dst, s0=s0, s1=s1: eng.tensor_add(
                dst(), s0(), s1()), [(S_["csd"], 1)])
            for k in cons:
                csd_at[32 * vh + k] = S_["csd"].n
            if t == 33:
                csd_final[vh] = S_["csd"].n

    def emit_den(vhd):
        T = turn_of_den[vhd]
        slot = turn_slot(T)
        if T >= 2:
            sem, v = turn_consumed(T - 2)
            r.wait("tensor", sem, v)
        r.wait("tensor", S_["exp"], 32 * (vhd + 1))
        r.wait("tensor", S_["csd"], csd_final[vhd])
        for qc in range(2):
            r.op("tensor", lambda eng, qc=qc, vhd=vhd, slot=slot: eng.matmul(
                slot[:, qc * 512:(qc + 1) * 512], onesmb[:, :],
                pslot(vhd)[:, qc * 512:(qc + 1) * 512],
                start=True, stop=True), [(S_["dnb"], 1)])

    def emit_recip(vhd):
        slot = turn_slot(turn_of_den[vhd])
        r.wait("vector", S_["dnb"], 2 * (vhd + 1))
        for h in range(2):
            r.op("vector", lambda eng, h=h, vhd=vhd, slot=slot: eng.reciprocal(
                obuf[1][:, (vhd % 2) * 1024 + h * 512:(vhd % 2) * 1024 + (h + 1) * 512],
                slot[:, h * 512:(h + 1) * 512]), [(S_["rcp"], 1)])

    def emit_evac(vhd, h):
        r.wait("vector", S_["rcp"], 2 * vhd + h + 1)
        r.wait("vector", S_["avf"], 32 * (vhd + 1))
        if vhd == 0 and h == 0:
            r.wait("vector", S_["sp"], P1_SPILLS)
        r.op("vector", lambda eng, h=h, vhd=vhd: eng.tensor_mul(
            attnT(vhd, h * 512, 512), psB[:, h * 512:(h + 1) * 512],
            obuf[1][:, (vhd % 2) * 1024 + h * 512:(vhd % 2) * 1024 + (h + 1) * 512]),
            [(S_["nrm"], 1)])

    def emit_expB(vh, kt):
        gk = 32 * vh + kt
        T = turn_of_kt[(vh, kt)]
        slot = turn_slot(T)
        r.wait("scalar", S_["sc"], gk + 1)
        if gk >= 8:
            r.wait("scalar", S_["avf"], gk - 7)
            r.wait("scalar", S_["csd"], csd_at[gk - 8])
        r.op("scalar", lambda eng, gk=gk, slot=slot: eng.activation(
            ring(gk), slot[:, :], AF.Exp), [(S_["exp"], 1)])

    def emit_scoresB(vh, kt):
        gk = 32 * vh + kt
        T = turn_of_kt[(vh, kt)]
        slot = turn_slot(T)
        qh, par = vh % 2, (vh // 2) % 2
        if kt == 0:
            r.wait("tensor", S_["xr"], 16 * (vh + 1))
        if T >= 2:
            sem, v = turn_consumed(T - 2)
            r.wait("tensor", sem, v)
        piece, tile = kt // 16, kt % 16
        for qc in range(2):
            r.op("tensor", lambda eng, qc=qc, qh=qh, slot=slot: eng.matmul(
                slot[:, qc * 512:(qc + 1) * 512], onesmb[32 * qh:32 * qh + 1, :],
                X[32 * qh:32 * qh + 1, qh * 1024 + qc * 512:qh * 1024 + (qc + 1) * 512],
                start=True, stop=False), [])
        for qc in range(2):
            r.op("tensor", lambda eng, piece=piece, tile=tile, qc=qc, par=par, qh=qh,
                 slot=slot:
                 eng.matmul(slot[:, qc * 512:(qc + 1) * 512],
                            ktbuf(par, piece)[:, tile * 128:(tile + 1) * 128],
                            qbuf(par)[:, qh * 1024 + qc * 512:qh * 1024 + (qc + 1) * 512],
                            start=False, stop=True),
                 [(S_["sc"], 1)] if qc == 1 else [])

    av_q = []
    for vhV in range(NVH):
        for ktV in range(32):
            av_q.append((vhV * 32 + max(6, ktV + 2), vhV, ktV))
    av_q.sort()

    def emit_av_pops(p, limit=2):
        n_av = 0
        while av_q and av_q[0][0] <= p and n_av < limit:
            _, vhV, ktV = av_q.pop(0)
            parV = (vhV // 2) % 2
            gkV = vhV * 32 + ktV
            r.wait("tensor", S_["exp"], gkV + 1)
            piece, tile = ktV // 16, ktV % 16
            for qc in range(2):
                if ktV == 0:
                    r.wait("tensor", S_["nrm"], 2 * (vhV - 1) + qc + 1)
                r.op("tensor", lambda eng, piece=piece, tile=tile, qc=qc, parV=parV,
                     gkV=gkV, ktV=ktV:
                     eng.matmul(psB[:, qc * 512:(qc + 1) * 512],
                                vbuf(parV, piece)[:, tile * 128:(tile + 1) * 128],
                                ring(gkV)[:, qc * 512:(qc + 1) * 512],
                                start=(ktV == 0), stop=(ktV == 31)),
                     [(S_["avf"], 1)] if qc == 1 else [])
            n_av += 1

    emit_loads(0)
    emit_loads(1)
    r.wait("tensor", S_["ld0"], lds0)
    for ga in range(96):
        emit_passA_mm(ga)
        emit_passA_dve(ga)
        if ga == 63:
            for j in range(8):
                emit_stat_dma(0, j)
            emit_xrow(0)
    for j in range(8):
        emit_stat_dma(1, j)
    emit_xrow(1)

    NPOS = NVH * 32
    for p in range(NPOS):
        vh, kt = p // 32, p % 32

        for hh in range(2, 16):
            if p == 64 * (hh - 1):
                emit_loads(hh)
        for vhg in range(2, NVH):
            for j in range(8):
                if p == 32 * (vhg - 2) + 20 + 4 * j:
                    emit_stat_dma(vhg, j)
            if p == 32 * (vhg - 1) + 18:
                emit_xrow(vhg)

        if kt == 2 and vh >= 1:
            emit_den(vh - 1)
        emit_scoresB(vh, kt)
        ga0 = 2 * (p + 48)
        if ga0 + 1 < 2048:
            emit_passA_mm(ga0)
            emit_passA_mm(ga0 + 1)
        emit_av_pops(p)

        emit_expB(vh, kt)

        if kt == 0 and vh >= 1:
            emit_cascade(vh - 1, 32)
        if kt == 1 and vh >= 1:
            emit_cascade(vh - 1, 33)
        if kt == 3 and vh >= 1:
            emit_recip(vh - 1)
        if kt == 4 and vh >= 1:
            emit_evac(vh - 1, 0)
        if kt == 5 and vh >= 1:
            emit_evac(vh - 1, 1)
        if ga0 + 1 < 2048:
            emit_passA_dve(ga0)
            emit_passA_dve(ga0 + 1)
        emit_cascade(vh, kt)

    emit_av_pops(10 ** 9, limit=64)
    emit_cascade(31, 32)
    emit_cascade(31, 33)
    emit_den(31)
    emit_recip(31)
    emit_evac(31, 0)
    emit_evac(31, 1)
    NRM_TOTAL = S_["nrm"].n

    psF_cm.__exit__(None, None, None)
    psC_cm.__exit__(None, None, None)
    psB_cm.__exit__(None, None, None)
    psD = e(nc.psum_tensor("psD", [128, 1024], F32))
    psT = e(nc.psum_tensor("psT", [128, 2048], BF16))
    psE = e(nc.psum_tensor("psE", [128, 1024], F32))

    slot_items = []
    for g in range(4):
        slot_items.append(("wo", g))
        slot_items.append(("xr", g))
    slot_of = {}
    slot_uses = [0, 0, 0]
    for i, item in enumerate(slot_items):
        slot_of[item] = i % 3

    def item_consumed(item):
        knd, g = item
        return (S_["op"], 16 * (g + 1)) if knd == "wo" else (S_["r1"], 16 * (g + 1))

    for i, (knd, g) in enumerate(slot_items):
        slot = i % 3
        if slot_uses[slot] == 0:
            r.wait("gpsimd", S_["mmP1"], P1_MM)
        else:
            sem, v = item_consumed(slot_items[i - 3])
            r.wait("gpsimd", sem, v)
        src = woT if knd == "wo" else xnat
        r.op("gpsimd", lambda eng, slot=slot, g=g, src=src: eng.dma_start(
            C_[slot][:, :], dap(src, g * 512, [[D, 128], [128 * D, 16], [1, 512]])),
            [(S_["c" + str(slot)], 16)])
        slot_uses[slot] += 1
    c_counts = {}
    cnt = [0, 0, 0]
    for i, item in enumerate(slot_items):
        slot = i % 3
        cnt[slot] += 16
        c_counts[item] = cnt[slot]

    r.wait("tensor", S_["nrm"], NRM_TOTAL)
    opidx = 0
    for g in range(4):
        swo = slot_of[("wo", g)]
        sxr = slot_of[("xr", g)]
        r.wait("tensor", S_["c" + str(swo)], c_counts[("wo", g)])
        for qt in range(16):
            half = opidx % 2
            r.wait("tensor", S_["r1"], opidx - 1)
            for dc in range(16):
                incs = [(S_["op"], 1)] if dc == 15 else []
                r.op("tensor", lambda eng, g=g, qt=qt, dc=dc, half=half, swo=swo:
                     eng.matmul(psD[:, half * 512:(half + 1) * 512],
                                B_[dc // 4][:, (dc % 4) * 2048 + qt * 128:(dc % 4) * 2048 + (qt + 1) * 128],
                                C_[swo][:, dc * 512:(dc + 1) * 512],
                                start=(dc == 0), stop=(dc == 15)), incs)
            r.wait("vector", S_["op"], opidx + 1)
            if qt == 0:
                r.wait("vector", S_["c" + str(sxr)], c_counts[("xr", g)])
            r.op("vector", lambda eng, g=g, qt=qt, half=half, sxr=sxr:
                 eng.tensor_add(A_[qt][:, g * 512:(g + 1) * 512],
                                psD[:, half * 512:(half + 1) * 512],
                                C_[sxr][:, qt * 512:(qt + 1) * 512]), [(S_["r1"], 1)])
            opidx += 1

    for qt in range(16):
        r.wait("vector", S_["r1"], 48 + qt + 1)
        r.op("vector", lambda eng, qt=qt: eng.reduce_sum(stcol[:, qt:qt + 1],
                                                         A_[qt][:, :], axis=AX.X),
             [(S_["st1"], 1)])
        r.wait("scalar", S_["r1"], 48 + qt + 1)
        r.wait("scalar", S_["sq1"], S_["sq1"].n)
        r.op("scalar", lambda eng, qt=qt: eng.activation(obuf[1][:, :], A_[qt][:, :],
                                                         AF.Square,
                                                         accum_out=stcol[:, 16 + qt:17 + qt]),
             [(S_["sq1"], 1)])

    def ln_finalize(r_s1, r_s2, r_a, r_b):
        def rng(base):
            return stcol[:, base:base + 16]
        r.op("vector", lambda eng: eng.tensor_scalar(rng(r_a), rng(r_s1),
                                                     float(1.0 / D), None, ALU.mult),
             [(S_["vch"], 1)])
        r.op("vector", lambda eng: eng.tensor_scalar(rng(r_b), rng(r_s2),
                                                     float(1.0 / D), None, ALU.mult),
             [(S_["vch"], 1)])
        r.wait("vector", S_["vch"], S_["vch"].n)
        r.op("vector", lambda eng: eng.tensor_mul(rng(r_s1), rng(r_a), rng(r_a)),
             [(S_["vch"], 1)])
        r.wait("vector", S_["vch"], S_["vch"].n)
        r.op("vector", lambda eng: eng.tensor_sub(rng(r_s2), rng(r_b), rng(r_s1)),
             [(S_["vch"], 1)])
        r.wait("scalar", S_["vch"], S_["vch"].n)
        r.op("scalar", lambda eng: eng.activation(rng(r_b), rng(r_s2), AF.Sqrt,
                                                  bias=np.float32(EPS)), [(S_["sch"], 1)])
        r.wait("vector", S_["sch"], S_["sch"].n)
        r.op("vector", lambda eng: eng.reciprocal(rng(r_s1), rng(r_b)), [(S_["vch"], 1)])
        r.wait("vector", S_["vch"], S_["vch"].n)
        r.op("vector", lambda eng: eng.tensor_mul(rng(r_b), rng(r_a), rng(r_s1)),
             [(S_["vch"], 1)])
        r.wait("vector", S_["vch"], S_["vch"].n)
        r.op("vector", lambda eng: eng.tensor_scalar(rng(r_s2), rng(r_b),
                                                     -1.0, None, ALU.mult),
             [(S_["vch"], 1)])

    r.wait("vector", S_["st1"], 16)
    r.wait("vector", S_["sq1"], 16)
    ln_finalize(0, 16, 32, 48)
    LN1_FIN = S_["vch"].n

    sem, v = item_consumed(("wo", 3))
    r.wait("gpsimd", sem, v)
    r.op("gpsimd", lambda eng: eng.dma_start(
        C_[0][:, :], dap(ATd, 0, [[NE * RK, 128], [128 * NE * RK, 16], [1, 512]])),
        [(S_["wF"], 16)])
    sem, v = item_consumed(("xr", 3))
    r.wait("gpsimd", sem, v)
    r.op("gpsimd", lambda eng: eng.dma_start(
        C_[1][:, :], dap(BTd, 0, [[D, 128], [128 * D, 4], [1, 2048]])), [(S_["bc"], 16)])

    ps_t_last = [None, None]
    mh_txes_after = {}
    tx_i = 0
    e8_count_at = {}

    def x1T_block(mc, qt):
        base = (mc % 4) * 2048 + qt * 128
        return B_[mc // 4][:, base:base + 128]

    for qt in range(16):
        r.wait("vector", S_["vch"], LN1_FIN)
        r.op("vector", lambda eng, qt=qt: eng.tensor_scalar(
            A_[qt][:, :], A_[qt][:, :], stcol[:, qt:qt + 1], stcol[:, 16 + qt:17 + qt],
            ALU.mult, ALU.add), [(S_["n1"], 1)])

        r.wait("tensor", S_["n1"], qt + 1)
        for mc in range(16):
            bank = tx_i % 2
            if ps_t_last[bank] is not None:
                sem, cntv = ps_t_last[bank]
                r.wait("tensor", sem, cntv)
            r.op("tensor", lambda eng, qt=qt, mc=mc, bank=bank:
                 eng.transpose(psT[:, bank * 1024:bank * 1024 + 128],
                               A_[qt][:, mc * 128:(mc + 1) * 128], identb[:, :]),
                 [(S_["tx"], 1)])
            r.wait("vector", S_["tx"], S_["tx"].n)
            r.op("vector", lambda eng, qt=qt, mc=mc, bank=bank:
                 eng.tensor_copy(x1T_block(mc, qt), psT[:, bank * 1024:bank * 1024 + 128]),
                 [(S_["txev"], 1)])
            ps_t_last[bank] = (S_["txev"], S_["txev"].n)
            tx_i += 1

        r.wait("tensor", S_["txev"], S_["txev"].n)
        if qt == 0:
            r.wait("tensor", S_["wF"], 16)
            r.wait("tensor", S_["bc"], 16)
            r.wait("tensor", S_["ld0g"], 16)
        hhalf = qt % 2
        r.wait("tensor", S_["mk"], qt - 1)
        r.wait("tensor", S_["lg"], qt - 1)
        for mc in range(16):
            r.op("tensor", lambda eng, qt=qt, mc=mc, hhalf=hhalf:
                 eng.matmul(psD[:, hhalf * 512:(hhalf + 1) * 512],
                            x1T_block(mc, qt), C_[0][:, mc * 512:(mc + 1) * 512],
                            start=(mc == 0), stop=(mc == 15)), [])
            r.op("tensor", lambda eng, qt=qt, mc=mc, hhalf=hhalf:
                 eng.matmul(psA[:, hhalf * 512:hhalf * 512 + 8],
                            x1T_block(mc, qt), wrtb[:, mc * 8:(mc + 1) * 8],
                            start=(mc == 0), stop=(mc == 15)),
                 [(S_["h"], 1)] if mc == 15 else [])

        r.wait("vector", S_["h"], qt + 1)
        if qt == 0:
            r.wait("vector", S_["ld0"], lds0)
        if qt >= 2:
            r.wait("vector", S_["sch"], e8_count_at[qt - 2])
        lg8 = lgts[:, (qt % 2) * 8:(qt % 2) * 8 + 8]
        r.op("vector", lambda eng, qt=qt, hhalf=hhalf, lg8=lg8:
             eng.tensor_add(lg8, psA[:, hhalf * 512:hhalf * 512 + 8], brs[:, :]),
             [(S_["lg"], 1)])
        if qt >= 1:
            r.wait("vector", S_["sch"], e8_count_at[qt - 1])
        r.wait("vector", S_["lg"], S_["lg"].n)
        r.op("vector", lambda eng, lg8=lg8: eng.reduce_max(smf[:, 16:17], lg8, axis=AX.X,
                                                           negate=True), [(S_["vch"], 1)])
        r.op("vector", lambda eng, lg8=lg8: eng.reduce_max(smf[:, 17:18], lg8, axis=AX.X),
             [(S_["vch"], 1)])
        r.wait("scalar", S_["vch"], S_["vch"].n)
        r.op("scalar", lambda eng, lg8=lg8: eng.activation(smf[:, 0:8], lg8, AF.Exp,
                                                           bias=smf[:, 16:17],
                                                           accum_out=smf[:, 18:19]),
             [(S_["sch"], 1)])
        e8_count_at[qt] = S_["sch"].n
        r.wait("vector", S_["sch"], S_["sch"].n)
        r.op("vector", lambda eng: eng.reciprocal(smf[:, 19:20], smf[:, 18:19]),
             [(S_["vch"], 1)])
        r.op("vector", lambda eng, lg8=lg8: eng.tensor_scalar(smf[:, 8:16], lg8,
                                                              smf[:, 17:18], None, ALU.is_ge),
             [(S_["vch"], 1)])
        r.wait("vector", S_["vch"], S_["vch"].n)
        r.op("vector", lambda eng, qt=qt: eng.tensor_scalar(coefT[:, qt * 8:(qt + 1) * 8],
                                                            smf[:, 8:16], smf[:, 19:20],
                                                            None, ALU.mult), [(S_["co"], 1)])

        r.wait("vector", S_["co"], qt + 1)
        if qt >= 2:
            r.wait("vector", S_["txes"], mh_txes_after[qt - 2])
        for ex in range(NE):
            r.op("vector", lambda eng, qt=qt, ex=ex, hhalf=hhalf:
                 eng.tensor_scalar(mh[:, hhalf * 512 + ex * 64: hhalf * 512 + (ex + 1) * 64],
                                   psD[:, hhalf * 512 + ex * 64: hhalf * 512 + (ex + 1) * 64],
                                   coefT[:, qt * 8 + ex:qt * 8 + ex + 1], None, ALU.mult),
                 [(S_["mk"], 1)] if ex == NE - 1 else [])

        r.wait("tensor", S_["mk"], qt + 1)
        for blk in range(4):
            bank = tx_i % 2
            if ps_t_last[bank] is not None:
                sem, cntv = ps_t_last[bank]
                r.wait("tensor", sem, cntv)
            r.op("tensor", lambda eng, qt=qt, blk=blk, bank=bank, hhalf=hhalf:
                 eng.transpose(psT[:, bank * 1024:bank * 1024 + 128],
                               mh[:, hhalf * 512 + blk * 128: hhalf * 512 + (blk + 1) * 128],
                               identb[:, :]), [(S_["tx"], 1)])
            if qt == 0 and blk == 0:
                r.wait("scalar", S_["r1"], 48)
            r.wait("scalar", S_["tx"], S_["tx"].n)
            r.op("scalar", lambda eng, qt=qt, blk=blk, bank=bank:
                 eng.copy(C_[2][:, blk * 2048 + qt * 128: blk * 2048 + (qt + 1) * 128],
                          psT[:, bank * 1024:bank * 1024 + 128]), [(S_["txes"], 1)])
            ps_t_last[bank] = (S_["txes"], S_["txes"].n)
            tx_i += 1
        mh_txes_after[qt] = S_["txes"].n

        r.wait("tensor", S_["txes"], S_["txes"].n)
        for oc in range(4):
            yidx = qt * 4 + oc
            ybank = yidx % 2
            r.wait("tensor", S_["x2"], yidx - 1)
            for pair in range(4):
                r.op("tensor", lambda eng, qt=qt, oc=oc, pair=pair, ybank=ybank:
                     eng.matmul(psE[:, ybank * 512:(ybank + 1) * 512],
                                C_[2][:, pair * 2048 + qt * 128: pair * 2048 + (qt + 1) * 128],
                                C_[1][:, pair * 2048 + oc * 512: pair * 2048 + (oc + 1) * 512],
                                start=(pair == 0), stop=(pair == 3)),
                     [(S_["yy"], 1)] if pair == 3 else [])
            r.wait("vector", S_["yy"], yidx + 1)
            r.op("vector", lambda eng, qt=qt, oc=oc, ybank=ybank:
                 eng.scalar_tensor_tensor(A_[qt][:, oc * 512:(oc + 1) * 512],
                                          psE[:, ybank * 512:(ybank + 1) * 512],
                                          1.0 / 64.0, A_[qt][:, oc * 512:(oc + 1) * 512],
                                          ALU.mult, ALU.add), [(S_["x2"], 1)])
        r.wait("vector", S_["x2"], 4 * (qt + 1))
        r.op("vector", lambda eng, qt=qt: eng.reduce_sum(stcol[:, 32 + qt:33 + qt],
                                                         A_[qt][:, :], axis=AX.X),
             [(S_["st1"], 1)])
        r.wait("scalar", S_["x2"], 4 * (qt + 1))
        r.wait("scalar", S_["sq1"], S_["sq1"].n)
        r.op("scalar", lambda eng, qt=qt: eng.activation(obuf[1][:, :], A_[qt][:, :],
                                                         AF.Square,
                                                         accum_out=stcol[:, 48 + qt:49 + qt]),
             [(S_["sq1"], 1)])

    r.wait("vector", S_["st1"], 32)
    r.wait("vector", S_["sq1"], 32)
    ln_finalize(32, 48, 0, 16)
    LN2_FIN = S_["vch"].n

    for qt in range(16):
        r.wait("vector", S_["vch"], LN2_FIN)
        r.wait("vector", S_["out"], 16 * (qt - 1))
        r.op("vector", lambda eng, qt=qt: eng.tensor_scalar(
            obuf[qt % 2][:, :], A_[qt][:, :], stcol[:, 32 + qt:33 + qt],
            stcol[:, 48 + qt:49 + qt], ALU.mult, ALU.add), [(S_["n2"], 1)])
        r.wait("sync", S_["n2"], qt + 1)
        r.wait("sync", S_["out"], 16 * qt)
        r.op("sync", lambda eng, qt=qt: eng.dma_start(y.ap()[qt * 128:(qt + 1) * 128, :],
                                                      obuf[qt % 2][:, :]), [(S_["out"], 16)])
    r.wait("sync", S_["out"], 16 * 16)

    r.check()

    with nc.Block() as block:
        @block.tensor
        def _(eng):
            for f in r.ops["tensor"]:
                f(eng)

        @block.scalar
        def _(eng):
            for f in r.ops["scalar"]:
                f(eng)

        @block.vector
        def _(eng):
            for f in r.ops["vector"]:
                f(eng)

        @block.sync
        def _(eng):
            for f in r.ops["sync"]:
                f(eng)

        @block.gpsimd
        def _(eng):
            for f in r.ops["gpsimd"]:
                f(eng)

    ctx.close()
    return nc


_NC_CACHE = None


def _get_nc():
    global _NC_CACHE
    if _NC_CACHE is None:
        _NC_CACHE = build_nc()
    return _NC_CACHE


def make_in_maps(L0_attn_x, L0_attn_wq, L0_attn_wk, L0_attn_wv, L0_attn_wo,
                 L0_ffn_A, L0_ffn_B, L0_ffn_Wr, L0_ffn_br):
    x = np.asarray(L0_attn_x, dtype=np.float32)
    shared = {
        "wqT": np.ascontiguousarray(np.asarray(L0_attn_wq, np.float32).T),
        "wkT": np.ascontiguousarray(np.asarray(L0_attn_wk, np.float32).T),
        "wvT": np.ascontiguousarray(np.asarray(L0_attn_wv, np.float32).T),
        "woT": np.ascontiguousarray(np.asarray(L0_attn_wo, np.float32).T),
        "ATd": np.ascontiguousarray(
            np.asarray(L0_ffn_A, np.float32).transpose(2, 0, 1).reshape(D, NE * RK)),
        "BTd": np.ascontiguousarray(
            np.asarray(L0_ffn_B, np.float32).transpose(0, 2, 1).reshape(4 * 128, D)),
        "WrTd": np.ascontiguousarray(np.asarray(L0_ffn_Wr, np.float32).T),
        "brb": np.ascontiguousarray(
            np.broadcast_to(np.asarray(L0_ffn_br, np.float32).reshape(1, NE), (128, NE))),
        "identd": np.eye(128, dtype=ml_dtypes.bfloat16),
        "onesmd": np.ones((128, 128), dtype=np.float32),
        "onesrd": np.ones((1, 128), dtype=ml_dtypes.bfloat16),
        "onesmbd": np.ones((128, 128), dtype=ml_dtypes.bfloat16),
    }
    in_maps = []
    for c in range(8):
        b, hf = c // 2, c % 2
        own = x[b, hf * NQ:(hf + 1) * NQ, :]
        oth = x[b, (1 - hf) * NQ:(2 - hf) * NQ, :]
        m = dict(shared)
        m["xTb"] = np.ascontiguousarray(np.concatenate([own.T, oth.T], axis=1))
        m["xnat"] = np.ascontiguousarray(own)
        in_maps.append(m)
    return in_maps


def kernel(**inputs):
    in_maps = make_in_maps(**inputs)
    nc = _get_nc()
    res = run_bass_kernel_spmd(nc, in_maps, core_ids=list(range(8)))
    out = np.empty((4, 4096, 2048), dtype=np.float32)
    for c in range(8):
        b, hf = c // 2, c % 2
        out[b, hf * NQ:(hf + 1) * NQ, :] = res.results[c]["y"]
    return out


# version 07
# speedup vs baseline: 1.1849x; 1.1098x over previous
import numpy as np
import ml_dtypes
import concourse.bass as bass
import concourse.mybir as mybir
from concourse.bass_utils import run_bass_kernel_spmd

F32 = mybir.dt.float32
BF16 = mybir.dt.bfloat16
AF = mybir.ActivationFunctionType
ALU = mybir.AluOpType
AX = mybir.AxisListType

D = 2048
S = 4096
NQ = 2048
H = 16
HD = 128
NE = 8
RK = 64
BETA = 9.5
EPS = 1e-5

ENGS = ("tensor", "scalar", "vector", "sync", "gpsimd")


class Sem:
    def __init__(self, h, name):
        self.h = h
        self.name = name
        self.n = 0


class Rec:
    def __init__(self):
        self.ops = {e: [] for e in ENGS}
        self.waits = []

    def wait(self, e, sem, val):
        if val <= 0:
            return
        self.waits.append((sem, val))
        self.ops[e].append(lambda eng, s=sem, v=val: eng.wait_ge(s.h, v))

    def op(self, e, build, incs=()):
        incs = tuple(incs)
        for s, a in incs:
            s.n += a
        assert len(incs) <= 1, "one sem update per instruction"
        attach, spill = incs[:1], incs[1:]

        def f(eng, build=build, attach=attach, spill=spill):
            inst = build(eng)
            for s, a in attach:
                inst = inst.then_inc(s.h, a)
            for s, a in spill:
                eng.sem_inc(s.h, a)
        self.ops[e].append(f)

    def check(self):
        for s, v in self.waits:
            assert v <= s.n, f"wait on {s.name} for {v} but total incs {s.n}"


def dap(t, off, dims):
    return bass.AP(t, off, [list(d) for d in dims])


def build_nc():
    nc = bass.Bass("TRN2", target_bir_lowering=False)

    xTb = nc.dram_tensor("xTb", [D, S], F32, kind="ExternalInput")
    xnat = nc.dram_tensor("xnat", [NQ, D], F32, kind="ExternalInput")
    wqT = nc.dram_tensor("wqT", [D, D], F32, kind="ExternalInput")
    wkT = nc.dram_tensor("wkT", [D, D], F32, kind="ExternalInput")
    wvT = nc.dram_tensor("wvT", [D, D], F32, kind="ExternalInput")
    woT = nc.dram_tensor("woT", [D, D], F32, kind="ExternalInput")
    ATd = nc.dram_tensor("ATd", [D, NE * RK], F32, kind="ExternalInput")
    BTd = nc.dram_tensor("BTd", [4 * 128, D], F32, kind="ExternalInput")
    WrTd = nc.dram_tensor("WrTd", [D, NE], F32, kind="ExternalInput")
    brb = nc.dram_tensor("brb", [128, NE], F32, kind="ExternalInput")
    identd = nc.dram_tensor("identd", [128, 128], BF16, kind="ExternalInput")
    onesmd = nc.dram_tensor("onesmd", [128, 128], F32, kind="ExternalInput")
    onesmbd = nc.dram_tensor("onesmbd", [128, 128], BF16, kind="ExternalInput")
    onesrd = nc.dram_tensor("onesrd", [1, 128], BF16, kind="ExternalInput")

    y = nc.dram_tensor("y", [NQ, D], F32, kind="ExternalOutput")

    kdram = nc.dram_tensor("kdram", [8, 128, 8192], BF16)
    vdram = nc.dram_tensor("vdram", [S, D], BF16)
    qdram = nc.dram_tensor("qdram", [4, 128, 8192], BF16)
    negdram = nc.dram_tensor("negdram", [16, 2048], BF16)

    from contextlib import ExitStack
    ctx = ExitStack()
    e = ctx.enter_context

    AB = [e(nc.sbuf_tensor(f"AB{i}", [128, 8192], BF16)) for i in range(4)]
    A_ = [AB[i // 4][:, (i % 4) * 2048:(i % 4 + 1) * 2048] for i in range(16)]
    B_ = [e(nc.sbuf_tensor(f"B{i}", [128, 8192], BF16)) for i in range(4)]
    C_ = [e(nc.sbuf_tensor(f"C{i}", [128, 8192], BF16)) for i in range(3)]
    obuf = [e(nc.sbuf_tensor(f"obuf{i}", [128, 2048], F32)) for i in range(2)]
    mh = e(nc.sbuf_tensor("mh", [128, 1024], BF16))
    coefT = e(nc.sbuf_tensor("coefT", [128, 128], F32))
    stcol = e(nc.sbuf_tensor("stcol", [128, 64], F32))
    smf = e(nc.sbuf_tensor("smf", [128, 32], F32))
    lgts = e(nc.sbuf_tensor("lgts", [128, 16], F32))
    wrtb = e(nc.sbuf_tensor("wrtb", [128, 128], BF16))
    brs = e(nc.sbuf_tensor("brs", [128, NE], F32))
    identb = e(nc.sbuf_tensor("identb", [128, 128], BF16))
    onesm = e(nc.sbuf_tensor("onesm", [128, 128], F32))
    onesr = e(nc.sbuf_tensor("onesr", [1, 128], BF16))
    onesmb = e(nc.sbuf_tensor("onesmb", [128, 128], BF16))
    acc2 = e(nc.sbuf_tensor("acc2", [128, 2048], BF16))
    pa = e(nc.sbuf_tensor("pa", [128, 32], F32))
    X = e(nc.sbuf_tensor("X", [128, 2048], BF16))

    psA = e(nc.psum_tensor("psA", [128, 1024], F32))
    psB_cm = nc.psum_tensor("psB", [128, 1024], F32)
    psB = psB_cm.__enter__()
    psC_cm = nc.psum_tensor("psC", [128, 1024], F32)
    psC = psC_cm.__enter__()
    psF_cm = nc.psum_tensor("psF", [128, 1024], F32)
    psF = psF_cm.__enter__()

    semnames = ("wA wAf x0 x1 xf0 xf1 mmP1 evP1 sp kv0 kv1 kvg0 kvg1 sc exp avf "
                "bc nrm c0 c1 c2 op r1 st1 sq1 vch sch n1 tx "
                "asc ard stq xrg xr csd dnb rcp lnr "
                "txev txes wF h lg co mk yy x2 n2 out ld0 ld0g").split()
    S_ = {n: Sem(e(nc.semaphore(f"s_{n}")), n) for n in semnames}

    r = Rec()

    for dst, src in ((identb, identd), (onesm, onesmd), (onesmb, onesmbd), (onesr, onesrd), (brs, brb)):
        r.wait("sync", S_["ld0"], S_["ld0"].n)
        r.op("sync", lambda eng, d=dst, s=src: eng.dma_start(d[:, :], s.ap()[:, :]),
             [(S_["ld0"], 16)])
    r.op("gpsimd", lambda eng: eng.dma_start(
        wrtb[:, :], dap(WrTd, 0, [[NE, 128], [128 * NE, 16], [1, NE]])), [(S_["ld0g"], 16)])
    lds0 = S_["ld0"].n

    def pool6(i):
        i = i % 6
        if i < 2:
            return psB[:, i * 512:(i + 1) * 512]
        if i < 4:
            return psC[:, (i - 2) * 512:(i - 2 + 1) * 512]
        return psF[:, (i - 4) * 512:(i - 4 + 1) * 512]

    g_grp = 0
    n_spills = 0
    vslot_i = 0
    stage_last_spill = {}
    xcnk_free_at = {}
    pass_end_mm = []

    passes = (("k", wkT, 8, kdram), ("v", wvT, 8, vdram), ("q", wqT, 4, qdram))
    for p_i, (kind, wT, nchunks, dest) in enumerate(passes):
        for g4 in range(4):
            if g4 == 0 and p_i > 0:
                r.wait("gpsimd", S_["mmP1"], pass_end_mm[p_i - 1])
            wsem = S_["wA"] if g4 % 2 == 0 else S_["wAf"]
            r.wait("gpsimd", wsem, wsem.n)
            r.op("gpsimd", lambda eng, g4=g4, wT=wT: eng.dma_start(
                AB[g4][:, :],
                dap(wT, g4 * 4 * 128 * D, [[D, 128], [128 * D, 4], [1, D]])),
                [(wsem, 16)])
        wA_after = (S_["wA"].n, S_["wAf"].n)

        for t in range(nchunks):
            par = t % 2
            if par in xcnk_free_at:
                r.wait("gpsimd", S_["mmP1"], xcnk_free_at[par])
            r.op("gpsimd", lambda eng, par=par, t=t: eng.dma_start(
                C_[par][:, :], dap(xTb, t * 512, [[S, 128], [128 * S, 16], [1, 512]])),
                [(S_["x" + str(par)], 16)])
            x_after = S_["x" + str(par)].n

            r.wait("tensor", S_["wA"], wA_after[0])
            r.wait("tensor", S_["wAf"], wA_after[1])
            r.wait("tensor", S_["x" + str(par)], x_after)

            if kind in ("k", "q"):
                for dk in range(16):
                    slot = g_grp % 6
                    r.wait("tensor", S_["evP1"], g_grp - 5)
                    for mc in range(16):
                        incs = [(S_["mmP1"], 1)] if mc == 15 else []
                        r.op("tensor", lambda eng, slot=slot, mc=mc, dk=dk, par=par:
                             eng.matmul(pool6(slot),
                                        A_[mc][:, dk * 128:(dk + 1) * 128],
                                        C_[par][:, mc * 512:(mc + 1) * 512],
                                        start=(mc == 0), stop=(mc == 15)), incs)
                    r.wait("scalar", S_["mmP1"], S_["mmP1"].n)
                    if dk == 0 and ("kq", par) in stage_last_spill:
                        r.wait("scalar", S_["sp"], 16 * stage_last_spill[("kq", par)])
                    r.op("scalar", lambda eng, slot=slot, dk=dk, par=par:
                         eng.copy(B_[par][:, dk * 512:(dk + 1) * 512], pool6(slot)),
                         [(S_["evP1"], 1)])
                    g_grp += 1
                r.wait("sync", S_["evP1"], S_["evP1"].n)
                r.wait("sync", S_["sp"], S_["sp"].n)
                r.op("sync", lambda eng, t=t, par=par, dest=dest:
                     eng.dma_start(dest.ap()[t, :, :], B_[par][:, :]), [(S_["sp"], 16)])
                n_spills += 1
                stage_last_spill[("kq", par)] = n_spills
            else:
                for tt in range(4):
                    vs = vslot_i % 4
                    tt_global = t * 4 + tt
                    for dvc in range(4):
                        slot = g_grp % 6
                        r.wait("tensor", S_["evP1"], g_grp - 5)
                        for mc in range(16):
                            incs = [(S_["mmP1"], 1)] if mc == 15 else []
                            r.op("tensor", lambda eng, slot=slot, mc=mc, tt=tt, dvc=dvc, par=par:
                                 eng.matmul(pool6(slot),
                                            C_[par][:, mc * 512 + tt * 128: mc * 512 + (tt + 1) * 128],
                                            A_[mc][:, dvc * 512:(dvc + 1) * 512],
                                            start=(mc == 0), stop=(mc == 15)), incs)
                        r.wait("scalar", S_["mmP1"], S_["mmP1"].n)
                        if dvc == 0 and ("v", vs) in stage_last_spill:
                            r.wait("scalar", S_["sp"], 16 * stage_last_spill[("v", vs)])
                        r.op("scalar", lambda eng, slot=slot, vs=vs, dvc=dvc:
                             eng.copy(B_[2][:, vs * 2048 + dvc * 512: vs * 2048 + (dvc + 1) * 512],
                                      pool6(slot)), [(S_["evP1"], 1)])
                        g_grp += 1
                    r.wait("sync", S_["evP1"], S_["evP1"].n)
                    r.wait("sync", S_["sp"], S_["sp"].n)
                    r.op("sync", lambda eng, tt_global=tt_global, vs=vs:
                         eng.dma_start(vdram.ap()[tt_global * 128:(tt_global + 1) * 128, :],
                                       B_[2][:, vs * 2048:(vs + 1) * 2048]), [(S_["sp"], 16)])
                    n_spills += 1
                    stage_last_spill[("v", vs)] = n_spills
                    vslot_i += 1
            xcnk_free_at[par] = S_["mmP1"].n
        pass_end_mm.append(S_["mmP1"].n)

    P1_SPILLS = S_["sp"].n
    P1_MM = S_["mmP1"].n
    EV_TOT = S_["evP1"].n

    def ktbuf(par, piece):
        return A_[2 * par + piece]

    def vbuf(par, piece):
        return A_[4 + 2 * par + piece]

    def qbuf(par):
        return A_[8 + par]

    def ring(i):
        return A_[10 + (i % 10) // 2][:, ((i % 10) % 2) * 1024:(((i % 10) % 2) + 1) * 1024]

    def attnT(vh, lo, width):
        hh, qh = vh // 2, vh % 2
        base = (hh % 4) * 2048 + qh * 1024 + lo
        return B_[hh // 4][:, base:base + width]

    def chslot(vh):
        return acc2[:, (vh % 2) * 1024:(vh % 2 + 1) * 1024]

    def pslot(vh):
        return acc2[:, ((vh + 1) % 2) * 1024:((vh + 1) % 2 + 1) * 1024]

    NVH = 32
    kv_snap = {}
    kvg_snap = {}

    def emit_loads(hh):
        par = hh % 2
        if hh < 2:
            r.wait("sync", S_["sp"], P1_SPILLS)
            r.wait("sync", S_["mmP1"], P1_MM)
            r.wait("gpsimd", S_["sp"], P1_SPILLS)
        else:
            r.wait("sync", S_["sc"], 64 * (hh - 1))
            r.wait("gpsimd", S_["avf"], 64 * (hh - 1))
        ks = S_["kv" + str(par)]
        gs = S_["kvg" + str(par)]
        for piece in range(2):
            r.wait("sync", ks, ks.n)
            r.op("sync", lambda eng, hh=hh, g=piece, par=par: eng.dma_start(
                ktbuf(par, g)[:, :],
                dap(kdram, (4 * g) * 128 * 8192 + hh * 512,
                    [[8192, 128], [128 * 8192, 4], [1, 512]])), [(ks, 16)])
            r.wait("gpsimd", gs, gs.n)
            r.op("gpsimd", lambda eng, hh=hh, g=piece, par=par: eng.dma_start(
                vbuf(par, g)[:, :],
                dap(vdram, (g * 16 * 128) * D + hh * 128,
                    [[D, 128], [128 * D, 16], [1, 128]])), [(gs, 16)])
        r.wait("sync", ks, ks.n)
        r.op("sync", lambda eng, hh=hh, par=par: eng.dma_start(
            qbuf(par)[:, :],
            dap(qdram, hh * 512, [[8192, 128], [128 * 8192, 4], [1, 512]])),
            [(ks, 16)])
        kv_snap[hh] = ks.n
        kvg_snap[hh] = gs.n

    turns = []
    turn_of_kt = {}
    turn_of_den = {}
    for vh in range(NVH):
        for kt in range(32):
            if kt == 4 and vh >= 1:
                turn_of_den[vh - 1] = len(turns)
                turns.append(("den", vh - 1))
            turn_of_kt[(vh, kt)] = len(turns)
            turns.append(("kt", vh, kt))
    turn_of_den[31] = len(turns)
    turns.append(("den", 31))

    def turn_slot(T):
        return psA if T % 2 == 0 else psC

    def turn_consumed(T):
        t = turns[T]
        if t[0] == "kt":
            return (S_["exp"], t[1] * 32 + t[2] + 1)
        return (S_["rcp"], t[1] + 1)

    def emit_passA_mm(ga):
        vh, a = ga // 64, ga % 64
        hh, qh, par = vh // 2, vh % 2, (vh // 2) % 2
        j, c = a // 8, a % 8
        if a == 0:
            r.wait("tensor", S_["kv" + str(par)], kv_snap[hh])
            r.wait("tensor", S_["kvg" + str(par)], kvg_snap[hh])
        if ga == 0:
            r.wait("tensor", S_["evP1"], EV_TOT)
        r.wait("tensor", S_["ard"], ga - 1)
        r.op("tensor", lambda eng, ga=ga, qh=qh, par=par, j=j, c=c: eng.matmul(
            psF[:, (ga % 2) * 512:(ga % 2 + 1) * 512],
            qbuf(par)[:, qh * 1024 + j * 128:qh * 1024 + (j + 1) * 128],
            ktbuf(par, c // 4)[:, (c % 4) * 512:(c % 4 + 1) * 512],
            start=True, stop=True), [(S_["asc"], 1)])

    def emit_passA_dve(ga):
        vh, a = ga // 64, ga % 64
        j, c = a // 8, a % 8
        scc = (j % 2) * 8 + c
        r.wait("vector", S_["asc"], ga + 1)
        r.op("vector", lambda eng, ga=ga, scc=scc: eng.reduce_max(
            pa[:, scc:scc + 1], psF[:, (ga % 2) * 512:(ga % 2 + 1) * 512],
            axis=AX.X), [(S_["ard"], 1)])
        if c == 7:
            stc = 16 + (vh % 2) * 8 + j
            r.wait("vector", S_["ard"], S_["ard"].n)
            if vh >= 2:
                r.wait("vector", S_["xrg"], 16 * (8 * (vh - 2) + j + 1))
            r.op("vector", lambda eng, j=j, stc=stc: eng.reduce_max(
                pa[:, stc:stc + 1], pa[:, (j % 2) * 8:(j % 2) * 8 + 8],
                axis=AX.X, negate=True), [(S_["stq"], 1)])

    def emit_stat_dma(vh, j):
        stc = 16 + (vh % 2) * 8 + j
        r.wait("gpsimd", S_["xrg"], S_["xrg"].n)
        r.wait("gpsimd", S_["stq"], 8 * vh + j + 1)
        r.op("gpsimd", lambda eng, vh=vh, j=j, stc=stc: eng.dma_start(
            dap(negdram, vh * 1024 + j * 128, [[1, 128]]),
            pa[:, stc:stc + 1]), [(S_["xrg"], 16)])

    def emit_xrow(vh):
        qh = vh % 2
        r.wait("sync", S_["xr"], S_["xr"].n)
        r.wait("sync", S_["xrg"], 16 * 8 * (vh + 1))
        if vh >= 2:
            r.wait("sync", S_["sc"], 32 * (vh - 1))
        r.op("sync", lambda eng, vh=vh, qh=qh: eng.dma_start(
            X[32 * qh:32 * qh + 1, qh * 1024:(qh + 1) * 1024],
            dap(negdram, vh * 1024, [[1024, 1], [1, 1024]])), [(S_["xr"], 16)])

    def casc_ops(vh, t):
        CH, P = (lambda: chslot(vh)), (lambda: pslot(vh))
        R = lambda k: (lambda: ring(32 * vh + k))
        if t == 5:
            return [(CH, R(0), R(1), [0, 1])]
        if 6 <= t <= 11:
            return [(CH, CH, R(t - 4), [t - 4])]
        if t == 12:
            return [(P, R(8), R(9), [8, 9])]
        if 13 <= t <= 18:
            return [(P, P, R(t - 3), [t - 3])]
        if t == 19:
            return [(P, P, CH, [])]
        if t == 20:
            return [(CH, R(16), R(17), [16, 17])]
        if 21 <= t <= 26:
            return [(CH, CH, R(t - 3), [t - 3])]
        if t == 27:
            return [(P, P, CH, [])]
        if t == 28:
            return [(CH, R(24), R(25), [24, 25])]
        if 29 <= t <= 33:
            return [(CH, CH, R(t - 3), [t - 3])]
        if t == 34:
            return [(CH, CH, R(31), [31])]
        if t == 35:
            return [(P, P, CH, [])]
        return []

    csd_at = {}
    csd_final = {}

    def emit_cascade(vh, t):
        for dst, s0, s1, cons in casc_ops(vh, t):
            if cons:
                r.wait("vector", S_["exp"], 32 * vh + max(cons) + 1)
            if t == 5:
                r.wait("vector", S_["dnb"], 2 * vh)
            r.wait("vector", S_["csd"], S_["csd"].n)
            r.op("vector", lambda eng, dst=dst, s0=s0, s1=s1: eng.tensor_add(
                dst(), s0(), s1()), [(S_["csd"], 1)])
            for k in cons:
                csd_at[32 * vh + k] = S_["csd"].n
            if t == 35:
                csd_final[vh] = S_["csd"].n

    def emit_den(vhd):
        T = turn_of_den[vhd]
        slot = turn_slot(T)
        if T >= 2:
            sem, v = turn_consumed(T - 2)
            r.wait("tensor", sem, v)
        r.wait("tensor", S_["exp"], 32 * (vhd + 1))
        r.wait("tensor", S_["csd"], csd_final[vhd])
        for qc in range(2):
            r.op("tensor", lambda eng, qc=qc, vhd=vhd, slot=slot: eng.matmul(
                slot[:, qc * 512:(qc + 1) * 512], onesmb[:, :],
                pslot(vhd)[:, qc * 512:(qc + 1) * 512],
                start=True, stop=True), [(S_["dnb"], 1)])

    def emit_recip(vhd):
        slot = turn_slot(turn_of_den[vhd])
        r.wait("scalar", S_["dnb"], 2 * (vhd + 1))
        r.wait("scalar", S_["rcp"], vhd - 1)
        r.op("scalar", lambda eng, vhd=vhd, slot=slot: eng.activation(
            obuf[0][:, (vhd % 2) * 1024:(vhd % 2 + 1) * 1024],
            slot[:, 0:1024], AF.Ln), [(S_["lnr"], 1)])
        r.wait("scalar", S_["lnr"], S_["lnr"].n)
        r.op("scalar", lambda eng, vhd=vhd: eng.activation(
            obuf[1][:, (vhd % 2) * 1024:(vhd % 2 + 1) * 1024],
            obuf[0][:, (vhd % 2) * 1024:(vhd % 2 + 1) * 1024],
            AF.Exp, scale=np.float32(-1.0)), [(S_["rcp"], 1)])

    def emit_evac(vhd, h):
        r.wait("vector", S_["rcp"], vhd + 1)
        r.wait("vector", S_["avf"], 32 * (vhd + 1))
        if vhd == 0 and h == 0:
            r.wait("vector", S_["sp"], P1_SPILLS)
        r.op("vector", lambda eng, h=h, vhd=vhd: eng.tensor_mul(
            attnT(vhd, h * 512, 512), psB[:, h * 512:(h + 1) * 512],
            obuf[1][:, (vhd % 2) * 1024 + h * 512:(vhd % 2) * 1024 + (h + 1) * 512]),
            [(S_["nrm"], 1)])

    def emit_expB(vh, kt):
        gk = 32 * vh + kt
        T = turn_of_kt[(vh, kt)]
        slot = turn_slot(T)
        r.wait("scalar", S_["sc"], gk + 1)
        if gk >= 10:
            r.wait("scalar", S_["avf"], gk - 9)
            r.wait("scalar", S_["csd"], csd_at[gk - 10])
        r.op("scalar", lambda eng, gk=gk, slot=slot: eng.activation(
            ring(gk), slot[:, :], AF.Exp), [(S_["exp"], 1)])

    def emit_scoresB(vh, kt):
        gk = 32 * vh + kt
        T = turn_of_kt[(vh, kt)]
        slot = turn_slot(T)
        qh, par = vh % 2, (vh // 2) % 2
        if kt == 0:
            r.wait("tensor", S_["xr"], 16 * (vh + 1))
        if T >= 2:
            sem, v = turn_consumed(T - 2)
            r.wait("tensor", sem, v)
        piece, tile = kt // 16, kt % 16
        for qc in range(2):
            r.op("tensor", lambda eng, qc=qc, qh=qh, slot=slot: eng.matmul(
                slot[:, qc * 512:(qc + 1) * 512], onesmb[32 * qh:32 * qh + 1, :],
                X[32 * qh:32 * qh + 1, qh * 1024 + qc * 512:qh * 1024 + (qc + 1) * 512],
                start=True, stop=False), [])
        for qc in range(2):
            r.op("tensor", lambda eng, piece=piece, tile=tile, qc=qc, par=par, qh=qh,
                 slot=slot:
                 eng.matmul(slot[:, qc * 512:(qc + 1) * 512],
                            ktbuf(par, piece)[:, tile * 128:(tile + 1) * 128],
                            qbuf(par)[:, qh * 1024 + qc * 512:qh * 1024 + (qc + 1) * 512],
                            start=False, stop=True),
                 [(S_["sc"], 1)] if qc == 1 else [])

    av_q = []
    for vhV in range(NVH):
        for ktV in range(32):
            av_q.append((vhV * 32 + max(9, ktV + 2), vhV, ktV))
    av_q.sort()

    def emit_av_pops(p, limit=2):
        n_av = 0
        while av_q and av_q[0][0] <= p and n_av < limit:
            _, vhV, ktV = av_q.pop(0)
            parV = (vhV // 2) % 2
            gkV = vhV * 32 + ktV
            r.wait("tensor", S_["exp"], gkV + 1)
            piece, tile = ktV // 16, ktV % 16
            for qc in range(2):
                if ktV == 0:
                    r.wait("tensor", S_["nrm"], 2 * (vhV - 1) + qc + 1)
                r.op("tensor", lambda eng, piece=piece, tile=tile, qc=qc, parV=parV,
                     gkV=gkV, ktV=ktV:
                     eng.matmul(psB[:, qc * 512:(qc + 1) * 512],
                                vbuf(parV, piece)[:, tile * 128:(tile + 1) * 128],
                                ring(gkV)[:, qc * 512:(qc + 1) * 512],
                                start=(ktV == 0), stop=(ktV == 31)),
                     [(S_["avf"], 1)] if qc == 1 else [])
            n_av += 1

    emit_loads(0)
    emit_loads(1)
    r.wait("tensor", S_["ld0"], lds0)
    for ga in range(96):
        emit_passA_mm(ga)
        emit_passA_dve(ga)
        if ga == 63:
            for j in range(8):
                emit_stat_dma(0, j)
            emit_xrow(0)
    for j in range(8):
        emit_stat_dma(1, j)
    emit_xrow(1)

    NPOS = NVH * 32
    for p in range(NPOS):
        vh, kt = p // 32, p % 32

        for hh in range(2, 16):
            if p == 64 * (hh - 1):
                emit_loads(hh)
        for vhg in range(2, NVH):
            for j in range(8):
                if p == 32 * (vhg - 2) + 20 + 4 * j:
                    emit_stat_dma(vhg, j)
            if p == 32 * (vhg - 1) + 18:
                emit_xrow(vhg)

        if kt == 4 and vh >= 1:
            emit_den(vh - 1)
        emit_scoresB(vh, kt)
        ga0 = 2 * (p + 48)
        if ga0 + 1 < 2048:
            emit_passA_mm(ga0)
            emit_passA_mm(ga0 + 1)
        emit_av_pops(p)

        if kt == 5 and vh >= 1:
            emit_recip(vh - 1)
        emit_expB(vh, kt)

        if kt in (0, 1, 2, 3) and vh >= 1:
            emit_cascade(vh - 1, 32 + kt)
        if kt == 6 and vh >= 1:
            emit_evac(vh - 1, 0)
        if kt == 7 and vh >= 1:
            emit_evac(vh - 1, 1)
        if ga0 + 1 < 2048:
            emit_passA_dve(ga0)
            emit_passA_dve(ga0 + 1)
        emit_cascade(vh, kt)

    emit_av_pops(10 ** 9, limit=64)
    for t in (32, 33, 34, 35):
        emit_cascade(31, t)
    emit_den(31)
    emit_recip(31)
    emit_evac(31, 0)
    emit_evac(31, 1)
    NRM_TOTAL = S_["nrm"].n

    psF_cm.__exit__(None, None, None)
    psC_cm.__exit__(None, None, None)
    psB_cm.__exit__(None, None, None)
    psD = e(nc.psum_tensor("psD", [128, 1024], F32))
    psT = e(nc.psum_tensor("psT", [128, 2048], BF16))
    psE = e(nc.psum_tensor("psE", [128, 1024], F32))

    slot_items = []
    for g in range(4):
        slot_items.append(("wo", g))
        slot_items.append(("xr", g))
    slot_of = {}
    slot_uses = [0, 0, 0]
    for i, item in enumerate(slot_items):
        slot_of[item] = i % 3

    def item_consumed(item):
        knd, g = item
        return (S_["op"], 16 * (g + 1)) if knd == "wo" else (S_["r1"], 16 * (g + 1))

    for i, (knd, g) in enumerate(slot_items):
        slot = i % 3
        if slot_uses[slot] == 0:
            r.wait("gpsimd", S_["mmP1"], P1_MM)
        else:
            sem, v = item_consumed(slot_items[i - 3])
            r.wait("gpsimd", sem, v)
        src = woT if knd == "wo" else xnat
        r.op("gpsimd", lambda eng, slot=slot, g=g, src=src: eng.dma_start(
            C_[slot][:, :], dap(src, g * 512, [[D, 128], [128 * D, 16], [1, 512]])),
            [(S_["c" + str(slot)], 16)])
        slot_uses[slot] += 1
    c_counts = {}
    cnt = [0, 0, 0]
    for i, item in enumerate(slot_items):
        slot = i % 3
        cnt[slot] += 16
        c_counts[item] = cnt[slot]

    r.wait("tensor", S_["nrm"], NRM_TOTAL)
    opidx = 0
    for g in range(4):
        swo = slot_of[("wo", g)]
        sxr = slot_of[("xr", g)]
        r.wait("tensor", S_["c" + str(swo)], c_counts[("wo", g)])
        for qt in range(16):
            half = opidx % 2
            r.wait("tensor", S_["r1"], opidx - 1)
            for dc in range(16):
                incs = [(S_["op"], 1)] if dc == 15 else []
                r.op("tensor", lambda eng, g=g, qt=qt, dc=dc, half=half, swo=swo:
                     eng.matmul(psD[:, half * 512:(half + 1) * 512],
                                B_[dc // 4][:, (dc % 4) * 2048 + qt * 128:(dc % 4) * 2048 + (qt + 1) * 128],
                                C_[swo][:, dc * 512:(dc + 1) * 512],
                                start=(dc == 0), stop=(dc == 15)), incs)
            r.wait("vector", S_["op"], opidx + 1)
            if qt == 0:
                r.wait("vector", S_["c" + str(sxr)], c_counts[("xr", g)])
            r.op("vector", lambda eng, g=g, qt=qt, half=half, sxr=sxr:
                 eng.tensor_add(A_[qt][:, g * 512:(g + 1) * 512],
                                psD[:, half * 512:(half + 1) * 512],
                                C_[sxr][:, qt * 512:(qt + 1) * 512]), [(S_["r1"], 1)])
            opidx += 1

    for qt in range(16):
        r.wait("vector", S_["r1"], 48 + qt + 1)
        r.op("vector", lambda eng, qt=qt: eng.reduce_sum(stcol[:, qt:qt + 1],
                                                         A_[qt][:, :], axis=AX.X),
             [(S_["st1"], 1)])
        r.wait("scalar", S_["r1"], 48 + qt + 1)
        r.wait("scalar", S_["sq1"], S_["sq1"].n)
        r.op("scalar", lambda eng, qt=qt: eng.activation(obuf[1][:, :], A_[qt][:, :],
                                                         AF.Square,
                                                         accum_out=stcol[:, 16 + qt:17 + qt]),
             [(S_["sq1"], 1)])

    def ln_finalize(r_s1, r_s2, r_a, r_b):
        def rng(base):
            return stcol[:, base:base + 16]
        r.op("vector", lambda eng: eng.tensor_scalar(rng(r_a), rng(r_s1),
                                                     float(1.0 / D), None, ALU.mult),
             [(S_["vch"], 1)])
        r.op("vector", lambda eng: eng.tensor_scalar(rng(r_b), rng(r_s2),
                                                     float(1.0 / D), None, ALU.mult),
             [(S_["vch"], 1)])
        r.wait("vector", S_["vch"], S_["vch"].n)
        r.op("vector", lambda eng: eng.tensor_mul(rng(r_s1), rng(r_a), rng(r_a)),
             [(S_["vch"], 1)])
        r.wait("vector", S_["vch"], S_["vch"].n)
        r.op("vector", lambda eng: eng.tensor_sub(rng(r_s2), rng(r_b), rng(r_s1)),
             [(S_["vch"], 1)])
        r.wait("scalar", S_["vch"], S_["vch"].n)
        r.op("scalar", lambda eng: eng.activation(rng(r_b), rng(r_s2), AF.Sqrt,
                                                  bias=np.float32(EPS)), [(S_["sch"], 1)])
        r.wait("vector", S_["sch"], S_["sch"].n)
        r.op("vector", lambda eng: eng.reciprocal(rng(r_s1), rng(r_b)), [(S_["vch"], 1)])
        r.wait("vector", S_["vch"], S_["vch"].n)
        r.op("vector", lambda eng: eng.tensor_mul(rng(r_b), rng(r_a), rng(r_s1)),
             [(S_["vch"], 1)])
        r.wait("vector", S_["vch"], S_["vch"].n)
        r.op("vector", lambda eng: eng.tensor_scalar(rng(r_s2), rng(r_b),
                                                     -1.0, None, ALU.mult),
             [(S_["vch"], 1)])

    r.wait("vector", S_["st1"], 16)
    r.wait("vector", S_["sq1"], 16)
    ln_finalize(0, 16, 32, 48)
    LN1_FIN = S_["vch"].n

    sem, v = item_consumed(("wo", 3))
    r.wait("gpsimd", sem, v)
    r.op("gpsimd", lambda eng: eng.dma_start(
        C_[0][:, :], dap(ATd, 0, [[NE * RK, 128], [128 * NE * RK, 16], [1, 512]])),
        [(S_["wF"], 16)])
    sem, v = item_consumed(("xr", 3))
    r.wait("gpsimd", sem, v)
    r.op("gpsimd", lambda eng: eng.dma_start(
        C_[1][:, :], dap(BTd, 0, [[D, 128], [128 * D, 4], [1, 2048]])), [(S_["bc"], 16)])

    ps_t_last = [None, None]
    mh_txes_after = {}
    tx_i = 0
    e8_count_at = {}

    def x1T_block(mc, qt):
        base = (mc % 4) * 2048 + qt * 128
        return B_[mc // 4][:, base:base + 128]

    for qt in range(16):
        r.wait("vector", S_["vch"], LN1_FIN)
        r.op("vector", lambda eng, qt=qt: eng.tensor_scalar(
            A_[qt][:, :], A_[qt][:, :], stcol[:, qt:qt + 1], stcol[:, 16 + qt:17 + qt],
            ALU.mult, ALU.add), [(S_["n1"], 1)])

        r.wait("tensor", S_["n1"], qt + 1)
        for mc in range(16):
            bank = tx_i % 2
            if ps_t_last[bank] is not None:
                sem, cntv = ps_t_last[bank]
                r.wait("tensor", sem, cntv)
            r.op("tensor", lambda eng, qt=qt, mc=mc, bank=bank:
                 eng.transpose(psT[:, bank * 1024:bank * 1024 + 128],
                               A_[qt][:, mc * 128:(mc + 1) * 128], identb[:, :]),
                 [(S_["tx"], 1)])
            r.wait("vector", S_["tx"], S_["tx"].n)
            r.op("vector", lambda eng, qt=qt, mc=mc, bank=bank:
                 eng.tensor_copy(x1T_block(mc, qt), psT[:, bank * 1024:bank * 1024 + 128]),
                 [(S_["txev"], 1)])
            ps_t_last[bank] = (S_["txev"], S_["txev"].n)
            tx_i += 1

        r.wait("tensor", S_["txev"], S_["txev"].n)
        if qt == 0:
            r.wait("tensor", S_["wF"], 16)
            r.wait("tensor", S_["bc"], 16)
            r.wait("tensor", S_["ld0g"], 16)
        hhalf = qt % 2
        r.wait("tensor", S_["mk"], qt - 1)
        r.wait("tensor", S_["lg"], qt - 1)
        for mc in range(16):
            r.op("tensor", lambda eng, qt=qt, mc=mc, hhalf=hhalf:
                 eng.matmul(psD[:, hhalf * 512:(hhalf + 1) * 512],
                            x1T_block(mc, qt), C_[0][:, mc * 512:(mc + 1) * 512],
                            start=(mc == 0), stop=(mc == 15)), [])
            r.op("tensor", lambda eng, qt=qt, mc=mc, hhalf=hhalf:
                 eng.matmul(psA[:, hhalf * 512:hhalf * 512 + 8],
                            x1T_block(mc, qt), wrtb[:, mc * 8:(mc + 1) * 8],
                            start=(mc == 0), stop=(mc == 15)),
                 [(S_["h"], 1)] if mc == 15 else [])

        r.wait("vector", S_["h"], qt + 1)
        if qt == 0:
            r.wait("vector", S_["ld0"], lds0)
        if qt >= 2:
            r.wait("vector", S_["sch"], e8_count_at[qt - 2])
        lg8 = lgts[:, (qt % 2) * 8:(qt % 2) * 8 + 8]
        r.op("vector", lambda eng, qt=qt, hhalf=hhalf, lg8=lg8:
             eng.tensor_add(lg8, psA[:, hhalf * 512:hhalf * 512 + 8], brs[:, :]),
             [(S_["lg"], 1)])
        if qt >= 1:
            r.wait("vector", S_["sch"], e8_count_at[qt - 1])
        r.wait("vector", S_["lg"], S_["lg"].n)
        r.op("vector", lambda eng, lg8=lg8: eng.reduce_max(smf[:, 16:17], lg8, axis=AX.X,
                                                           negate=True), [(S_["vch"], 1)])
        r.op("vector", lambda eng, lg8=lg8: eng.reduce_max(smf[:, 17:18], lg8, axis=AX.X),
             [(S_["vch"], 1)])
        r.wait("scalar", S_["vch"], S_["vch"].n)
        r.op("scalar", lambda eng, lg8=lg8: eng.activation(smf[:, 0:8], lg8, AF.Exp,
                                                           bias=smf[:, 16:17],
                                                           accum_out=smf[:, 18:19]),
             [(S_["sch"], 1)])
        e8_count_at[qt] = S_["sch"].n
        r.wait("vector", S_["sch"], S_["sch"].n)
        r.op("vector", lambda eng: eng.reciprocal(smf[:, 19:20], smf[:, 18:19]),
             [(S_["vch"], 1)])
        r.op("vector", lambda eng, lg8=lg8: eng.tensor_scalar(smf[:, 8:16], lg8,
                                                              smf[:, 17:18], None, ALU.is_ge),
             [(S_["vch"], 1)])
        r.wait("vector", S_["vch"], S_["vch"].n)
        r.op("vector", lambda eng, qt=qt: eng.tensor_scalar(coefT[:, qt * 8:(qt + 1) * 8],
                                                            smf[:, 8:16], smf[:, 19:20],
                                                            None, ALU.mult), [(S_["co"], 1)])

        r.wait("vector", S_["co"], qt + 1)
        if qt >= 2:
            r.wait("vector", S_["txes"], mh_txes_after[qt - 2])
        for ex in range(NE):
            r.op("vector", lambda eng, qt=qt, ex=ex, hhalf=hhalf:
                 eng.tensor_scalar(mh[:, hhalf * 512 + ex * 64: hhalf * 512 + (ex + 1) * 64],
                                   psD[:, hhalf * 512 + ex * 64: hhalf * 512 + (ex + 1) * 64],
                                   coefT[:, qt * 8 + ex:qt * 8 + ex + 1], None, ALU.mult),
                 [(S_["mk"], 1)] if ex == NE - 1 else [])

        r.wait("tensor", S_["mk"], qt + 1)
        for blk in range(4):
            bank = tx_i % 2
            if ps_t_last[bank] is not None:
                sem, cntv = ps_t_last[bank]
                r.wait("tensor", sem, cntv)
            r.op("tensor", lambda eng, qt=qt, blk=blk, bank=bank, hhalf=hhalf:
                 eng.transpose(psT[:, bank * 1024:bank * 1024 + 128],
                               mh[:, hhalf * 512 + blk * 128: hhalf * 512 + (blk + 1) * 128],
                               identb[:, :]), [(S_["tx"], 1)])
            if qt == 0 and blk == 0:
                r.wait("scalar", S_["r1"], 48)
            r.wait("scalar", S_["tx"], S_["tx"].n)
            r.op("scalar", lambda eng, qt=qt, blk=blk, bank=bank:
                 eng.copy(C_[2][:, blk * 2048 + qt * 128: blk * 2048 + (qt + 1) * 128],
                          psT[:, bank * 1024:bank * 1024 + 128]), [(S_["txes"], 1)])
            ps_t_last[bank] = (S_["txes"], S_["txes"].n)
            tx_i += 1
        mh_txes_after[qt] = S_["txes"].n

        r.wait("tensor", S_["txes"], S_["txes"].n)
        for oc in range(4):
            yidx = qt * 4 + oc
            ybank = yidx % 2
            r.wait("tensor", S_["x2"], yidx - 1)
            for pair in range(4):
                r.op("tensor", lambda eng, qt=qt, oc=oc, pair=pair, ybank=ybank:
                     eng.matmul(psE[:, ybank * 512:(ybank + 1) * 512],
                                C_[2][:, pair * 2048 + qt * 128: pair * 2048 + (qt + 1) * 128],
                                C_[1][:, pair * 2048 + oc * 512: pair * 2048 + (oc + 1) * 512],
                                start=(pair == 0), stop=(pair == 3)),
                     [(S_["yy"], 1)] if pair == 3 else [])
            r.wait("vector", S_["yy"], yidx + 1)
            r.op("vector", lambda eng, qt=qt, oc=oc, ybank=ybank:
                 eng.scalar_tensor_tensor(A_[qt][:, oc * 512:(oc + 1) * 512],
                                          psE[:, ybank * 512:(ybank + 1) * 512],
                                          1.0 / 64.0, A_[qt][:, oc * 512:(oc + 1) * 512],
                                          ALU.mult, ALU.add), [(S_["x2"], 1)])
        r.wait("vector", S_["x2"], 4 * (qt + 1))
        r.op("vector", lambda eng, qt=qt: eng.reduce_sum(stcol[:, 32 + qt:33 + qt],
                                                         A_[qt][:, :], axis=AX.X),
             [(S_["st1"], 1)])
        r.wait("scalar", S_["x2"], 4 * (qt + 1))
        r.wait("scalar", S_["sq1"], S_["sq1"].n)
        r.op("scalar", lambda eng, qt=qt: eng.activation(obuf[1][:, :], A_[qt][:, :],
                                                         AF.Square,
                                                         accum_out=stcol[:, 48 + qt:49 + qt]),
             [(S_["sq1"], 1)])

    r.wait("vector", S_["st1"], 32)
    r.wait("vector", S_["sq1"], 32)
    ln_finalize(32, 48, 0, 16)
    LN2_FIN = S_["vch"].n

    for qt in range(16):
        r.wait("vector", S_["vch"], LN2_FIN)
        r.wait("vector", S_["out"], 16 * (qt - 1))
        r.op("vector", lambda eng, qt=qt: eng.tensor_scalar(
            obuf[qt % 2][:, :], A_[qt][:, :], stcol[:, 32 + qt:33 + qt],
            stcol[:, 48 + qt:49 + qt], ALU.mult, ALU.add), [(S_["n2"], 1)])
        r.wait("sync", S_["n2"], qt + 1)
        r.wait("sync", S_["out"], 16 * qt)
        r.op("sync", lambda eng, qt=qt: eng.dma_start(y.ap()[qt * 128:(qt + 1) * 128, :],
                                                      obuf[qt % 2][:, :]), [(S_["out"], 16)])
    r.wait("sync", S_["out"], 16 * 16)

    r.check()

    with nc.Block() as block:
        @block.tensor
        def _(eng):
            for f in r.ops["tensor"]:
                f(eng)

        @block.scalar
        def _(eng):
            for f in r.ops["scalar"]:
                f(eng)

        @block.vector
        def _(eng):
            for f in r.ops["vector"]:
                f(eng)

        @block.sync
        def _(eng):
            for f in r.ops["sync"]:
                f(eng)

        @block.gpsimd
        def _(eng):
            for f in r.ops["gpsimd"]:
                f(eng)

    ctx.close()
    return nc


_NC_CACHE = None


def _get_nc():
    global _NC_CACHE
    if _NC_CACHE is None:
        _NC_CACHE = build_nc()
    return _NC_CACHE


def make_in_maps(L0_attn_x, L0_attn_wq, L0_attn_wk, L0_attn_wv, L0_attn_wo,
                 L0_ffn_A, L0_ffn_B, L0_ffn_Wr, L0_ffn_br):
    x = np.asarray(L0_attn_x, dtype=np.float32)
    shared = {
        "wqT": np.ascontiguousarray(np.asarray(L0_attn_wq, np.float32).T),
        "wkT": np.ascontiguousarray(np.asarray(L0_attn_wk, np.float32).T),
        "wvT": np.ascontiguousarray(np.asarray(L0_attn_wv, np.float32).T),
        "woT": np.ascontiguousarray(np.asarray(L0_attn_wo, np.float32).T),
        "ATd": np.ascontiguousarray(
            np.asarray(L0_ffn_A, np.float32).transpose(2, 0, 1).reshape(D, NE * RK)),
        "BTd": np.ascontiguousarray(
            np.asarray(L0_ffn_B, np.float32).transpose(0, 2, 1).reshape(4 * 128, D)),
        "WrTd": np.ascontiguousarray(np.asarray(L0_ffn_Wr, np.float32).T),
        "brb": np.ascontiguousarray(
            np.broadcast_to(np.asarray(L0_ffn_br, np.float32).reshape(1, NE), (128, NE))),
        "identd": np.eye(128, dtype=ml_dtypes.bfloat16),
        "onesmd": np.ones((128, 128), dtype=np.float32),
        "onesrd": np.ones((1, 128), dtype=ml_dtypes.bfloat16),
        "onesmbd": np.ones((128, 128), dtype=ml_dtypes.bfloat16),
    }
    in_maps = []
    for c in range(8):
        b, hf = c // 2, c % 2
        own = x[b, hf * NQ:(hf + 1) * NQ, :]
        oth = x[b, (1 - hf) * NQ:(2 - hf) * NQ, :]
        m = dict(shared)
        m["xTb"] = np.ascontiguousarray(np.concatenate([own.T, oth.T], axis=1))
        m["xnat"] = np.ascontiguousarray(own)
        in_maps.append(m)
    return in_maps


def kernel(**inputs):
    in_maps = make_in_maps(**inputs)
    nc = _get_nc()
    res = run_bass_kernel_spmd(nc, in_maps, core_ids=list(range(8)))
    out = np.empty((4, 4096, 2048), dtype=np.float32)
    for c in range(8):
        b, hf = c // 2, c % 2
        out[b, hf * NQ:(hf + 1) * NQ, :] = res.results[c]["y"]
    return out
